# version 26
# baseline (speedup 1.0000x reference)
import contextlib

import numpy as np
import concourse.bass as bass
import concourse.mybir as mybir
from concourse.bass_types import DynSlice
from concourse.bass_utils import run_bass_kernel_spmd

F32 = mybir.dt.float32
BF16 = mybir.dt.bfloat16
AF = mybir.ActivationFunctionType
OP = mybir.AluOpType
AX = mybir.AxisListType

B, S, D, N, RK = 2, 8192, 1024, 16, 64
NCORE = 8
T = B * S
OWN = T // NCORE
TBK = 2048
NBK = T // TBK
QT = 4096
RND = 12582912.0
EPS = 1e-5
ENGS = ("sync", "tensor", "scalar", "vector", "gpsimd")
DEBUG = False
DBGBLK = 0


def build_program():
    nc = bass.Bass("TRN2", target_bir_lowering=False)

    xtf = nc.dram_tensor("xtf", [D, T], F32, kind="ExternalInput").ap()
    xtd = nc.dram_tensor("xtd", [128, T], F32, kind="ExternalInput").ap()
    xown = nc.dram_tensor("xown", [OWN, D], F32, kind="ExternalInput").ap()
    wxp = nc.dram_tensor("wxp", [D, 96], F32, kind="ExternalInput").ap()
    wdtd = nc.dram_tensor("wdtd", [RK, 128], F32, kind="ExternalInput").ap()
    bdtd = nc.dram_tensor("bdtd", [128, 1], F32, kind="ExternalInput").ap()
    alogd = nc.dram_tensor("alogd", [128, N], F32, kind="ExternalInput").ap()
    rhs = nc.dram_tensor("rhs", [D, D], F32, kind="ExternalInput").ap()
    rhsT = nc.dram_tensor("rhsT", [D, D], F32, kind="ExternalInput").ap()
    ident = nc.dram_tensor("ident", [128, 128], F32, kind="ExternalInput").ap()
    bcsel = nc.dram_tensor("bcsel", [32, 32, 128], F32, kind="ExternalInput").ap()
    out = nc.dram_tensor("out", [OWN, D], F32, kind="ExternalOutput").ap()
    if DEBUG:
        d_xp = nc.dram_tensor("d_xp", [96, TBK], F32, kind="ExternalOutput").ap()
        d_dt = nc.dram_tensor("d_dt", [128, TBK], F32, kind="ExternalOutput").ap()
        d_u = nc.dram_tensor("d_u", [128, TBK], F32, kind="ExternalOutput").ap()
        d_ab = nc.dram_tensor("d_ab", [128, TBK], F32, kind="ExternalOutput").ap()
        d_bu = nc.dram_tensor("d_bu", [128, TBK], F32, kind="ExternalOutput").ap()
        d_h = nc.dram_tensor("d_h", [128, TBK], F32, kind="ExternalOutput").ap()
        d_yacc = nc.dram_tensor("d_yacc", [128, TBK], F32, kind="ExternalOutput").ap()
        d_ytc = nc.dram_tensor("d_ytc", [128, D], F32, kind="ExternalOutput").ap()
        d_z1 = nc.dram_tensor("d_z1", [128, D], F32, kind="ExternalOutput").ap()
        d_z1n = nc.dram_tensor("d_z1n", [128, D], F32, kind="ExternalOutput").ap()
        d_ql = nc.dram_tensor("d_ql", [128, D], F32, kind="ExternalOutput").ap()
        d_ffn = nc.dram_tensor("d_ffn", [128, D], F32, kind="ExternalOutput").ap()
        d_cols = nc.dram_tensor("d_cols", [128, 16], F32, kind="ExternalOutput").ap()
        d_qrhs = nc.dram_tensor("d_qrhs", [128, 8, D], F32, kind="ExternalOutput").ap()
        d_rsbc = nc.dram_tensor("d_rsbc", [128, D], F32, kind="ExternalOutput").ap()
    ybin_h = nc.dram_tensor("ybin", [NBK, 128, TBK], F32)
    ybout_h = nc.dram_tensor("ybout", [NBK, D, TBK], F32)
    yown_h = nc.dram_tensor("yown", [D, OWN], F32)
    ybin = ybin_h.ap()
    ybout = ybout_h.ap()
    yown = yown_h.ap()

    ops = {e: [] for e in ENGS}
    cnt = {c: 0 for c in ("din", "g", "cc", "t", "s", "v", "dout")}

    def emit(e, fn):
        ops[e].append(fn)

    def wait(e, c, val=None):
        v = cnt[c] if val is None else val
        if v > 0:
            emit(e, lambda E, SEM, c=c, v=v: E.wait_ge(SEM[c], v))

    def run(e, c, fn, n=1):
        emit(e, lambda E, SEM, fn=fn, c=c, n=n: fn(E).then_inc(SEM[c], n))
        cnt[c] += n

    def vop(fn):
        emit("vector", lambda E, SEM, fn=fn: fn(E))

    def vdrain():
        emit("vector", lambda E, SEM: E.drain(fusable=False))
        emit("vector", lambda E, SEM: E.sem_inc(SEM["v"], 1))
        cnt["v"] += 1

    with contextlib.ExitStack() as ctx:
        ent = ctx.enter_context
        wxp_sb = ent(nc.sbuf_tensor("wxp_sb", [128, 8, 96], BF16))
        wdt_sb = ent(nc.sbuf_tensor("wdt_sb", [RK, 128], BF16))
        bdt_sb = ent(nc.sbuf_tensor("bdt_sb", [128, 1], F32))
        alog_sb = ent(nc.sbuf_tensor("alog_sb", [128, N], F32))
        a_sb = ent(nc.sbuf_tensor("a_sb", [128, N], F32))
        ident_sb = ent(nc.sbuf_tensor("ident_sb", [128, 128], F32))
        bcsel_sb = ent(nc.sbuf_tensor("bcsel_sb", [96, 32, 128], BF16))
        bcself_sb = ent(nc.sbuf_tensor("bcself_sb", [72, 8, 128], F32))
        rs_col = ent(nc.sbuf_tensor("rs_col", [128, 8], F32))
        rr_col = ent(nc.sbuf_tensor("rr_col", [128, 8], F32))
        rsrow_tmp = ent(nc.sbuf_tensor("rsrow_tmp", [8, 2, 128], F32))
        rsrow_sb = ent(nc.sbuf_tensor("rsrow_sb", [72, 2, 128], F32))
        rs_bc = ent(nc.sbuf_tensor("rs_bc", [128, D], F32))
        rr_bc = ent(nc.sbuf_tensor("rr_bc", [128, D], F32))
        qrhs_sb = ent(nc.sbuf_tensor("qrhs_sb", [128, 8, D], BF16))
        xtb_sb = ent(nc.sbuf_tensor("xtb_sb", [128, 2, TBK], BF16))
        xtd_sb = ent(nc.sbuf_tensor("xtd_sb", [128, 1, TBK], F32))
        xp_sb = ent(nc.sbuf_tensor("xp_sb", [96, 3, TBK], BF16))
        dt_sb = ent(nc.sbuf_tensor("dt_sb", [128, 3, TBK], F32))
        u_sb = ent(nc.sbuf_tensor("u_sb", [128, 3, TBK], F32))
        ab_sb = ent(nc.sbuf_tensor("ab_sb", [128, 2, TBK], F32))
        bu_sb = ent(nc.sbuf_tensor("bu_sb", [128, TBK], F32))
        h_sb = ent(nc.sbuf_tensor("h_sb", [128, 2, TBK], F32))
        yacc_sb = ent(nc.sbuf_tensor("yacc_sb", [128, 2, TBK], F32))
        state_sb = ent(nc.sbuf_tensor("state_sb", [128, N], F32))
        ytc_sb = ent(nc.sbuf_tensor("ytc_sb", [128, 2, D], F32))
        xstg_sb = ent(nc.sbuf_tensor("xstg_sb", [128, 2, D], F32))
        z1_sb = ent(nc.sbuf_tensor("z1_sb", [128, D], F32))
        z2_sb = ent(nc.sbuf_tensor("z2_sb", [128, D], F32))
        ql_sb = ent(nc.sbuf_tensor("ql_sb", [128, D], F32))
        qlT_sb = ent(nc.sbuf_tensor("qlT_sb", [128, 8, 128], BF16))
        ffn_sb = ent(nc.sbuf_tensor("ffn_sb", [128, D], F32))
        cols_sb = ent(nc.sbuf_tensor("cols_sb", [128, 2, 20], F32))
        bnst_sb = ent(nc.sbuf_tensor("bnst_sb", [128, 2, 12], F32))
        eps_sb = ent(nc.sbuf_tensor("eps_sb", [128, 1], F32))
        one_sb = ent(nc.sbuf_tensor("one_sb", [128, 1], F32))
        rnd_sb = ent(nc.sbuf_tensor("rnd_sb", [128, 1], F32))
        nrnd_sb = ent(nc.sbuf_tensor("nrnd_sb", [128, 1], F32))
        ps_proj = ent(nc.psum_tensor("ps_proj", [128, 1024], F32))
        ps_bm = ent(nc.psum_tensor("ps_bm", [128, 1024], F32))
        ps_cb = ent(nc.psum_tensor("ps_cb", [128, 1024], F32))
        ps_tr = ent(nc.psum_tensor("ps_tr", [128, 512], F32))
        ps_f = ent(nc.psum_tensor("ps_f", [128, 512], F32))
        sm_din = ent(nc.semaphore("sm_din"))
        sm_g = ent(nc.semaphore("sm_g"))
        sm_cc = ent(nc.semaphore("sm_cc"))
        sm_t = ent(nc.semaphore("sm_t"))
        sm_s = ent(nc.semaphore("sm_s"))
        sm_v = ent(nc.semaphore("sm_v"))
        sm_dout = ent(nc.semaphore("sm_dout"))
        blk_ = ent(nc.Block())
        SEMH = {"din": sm_din, "g": sm_g, "cc": sm_cc, "t": sm_t, "s": sm_s,
                "v": sm_v, "dout": sm_dout}

        REG = {}

        def initreg(E, SEM):
            pid = E.partition_id()
            rbv = E.compute_val(pid * (D * TBK))
            REG["rb"] = rbv
            REG["s"] = E.alloc_register("yoffreg")
        emit("gpsimd", initreg)
        run("gpsimd", "g", lambda E: E.dma_start(
            out=wxp_sb[:, :, :], in_=wxp.rearrange("(c p) f -> p c f", p=128)), 16)
        run("gpsimd", "g", lambda E: E.dma_start(out=wdt_sb[:, :], in_=wdtd[:, :]), 16)
        run("gpsimd", "g", lambda E: E.dma_start(
            out=bcsel_sb[64:96, :, :], in_=bcsel[:, :, :]), 16)
        run("sync", "din", lambda E: E.dma_start(out=bdt_sb[:, :], in_=bdtd[:, :]), 16)
        run("sync", "din", lambda E: E.dma_start(out=alog_sb[:, :], in_=alogd[:, :]), 16)
        run("sync", "din", lambda E: E.dma_start(out=ident_sb[:, :], in_=ident[:, :]), 16)
        run("sync", "din", lambda E: E.dma_start(
            out=bcself_sb[64:72, :, :], in_=bcsel[0:8, 0:8, :]), 16)
        setup_din = cnt["din"]
        setup_g = cnt["g"]

        wait("scalar", "din", setup_din)
        run("scalar", "s", lambda E: E.activation(
            out=a_sb[:, :], in_=alog_sb[:, :], func=AF.Exp))
        wait("vector", "s")
        run("vector", "v", lambda E: E.tensor_scalar(
            out=a_sb[:, :], in0=a_sb[:, :], scalar1=-1.0, scalar2=None, op0=OP.mult))
        vdrain()
        v_neg = cnt["v"]

        for g in range(8):
            wait("sync", "v")
            run("sync", "din", lambda E, g=g: E.dma_start(
                out=ffn_sb[:, :], in_=rhsT[g * 128:(g + 1) * 128, :]), 16)
            wait("vector", "din")
            vop(lambda E, g=g: E.tensor_reduce(
                out=rs_col[:, g:g + 1], in_=ffn_sb[:, :], axis=AX.X, op=OP.max,
                apply_absolute_value=True))
            vdrain()
        vop(lambda E: E.tensor_scalar(
            out=rs_col[:, :], in0=rs_col[:, :], scalar1=1e-12, scalar2=1.0 / 127.0,
            op0=OP.max, op1=OP.mult))
        vdrain()
        vop(lambda E: E.reciprocal(out=rr_col[:, :], in_=rs_col[:, :]))
        vdrain()
        v_rs_done = cnt["v"]
        wait("tensor", "din", setup_din)
        wait("tensor", "v", v_rs_done)
        run("tensor", "t", lambda E: E.transpose(
            out=ps_tr[:8, 0:128], in_=rs_col[:, :], identity=ident_sb[:, :]))
        run("tensor", "t", lambda E: E.transpose(
            out=ps_tr[:8, 128:256], in_=rr_col[:, :], identity=ident_sb[:, :]))
        t_rows = cnt["t"]
        wait("vector", "t", t_rows)
        vop(lambda E: E.tensor_copy(out=rsrow_tmp[:, 0, :], in_=ps_tr[:8, 0:128]))
        vop(lambda E: E.tensor_copy(out=rsrow_tmp[:, 1, :], in_=ps_tr[:8, 128:256]))
        vdrain()
        v_rows = cnt["v"]
        wait("sync", "v", v_rows)
        run("sync", "din", lambda E: E.dma_start(
            out=rsrow_sb[64:72, :, :], in_=rsrow_tmp[:, :, :]), 16)
        rows_din = cnt["din"]
        wait("tensor", "din", rows_din)
        for g in range(8):
            run("tensor", "t", lambda E, g=g: E.matmul(
                out=ps_bm[:, g * 128:(g + 1) * 128] if g < 8 else None,
                lhsT=bcself_sb[64:72, g, :],
                rhs=rsrow_sb[64:72, 0, :], start=True, stop=True))
        for g in range(8):
            run("tensor", "t", lambda E, g=g: E.matmul(
                out=ps_cb[:, g * 128:(g + 1) * 128], lhsT=bcself_sb[64:72, g, :],
                rhs=rsrow_sb[64:72, 1, :], start=True, stop=True))
        t_bc = cnt["t"]
        wait("vector", "t", t_bc)
        vop(lambda E: E.tensor_copy(out=rs_bc[:, 0:1024], in_=ps_bm[:, 0:1024]))
        vop(lambda E: E.tensor_copy(out=rr_bc[:, 0:1024], in_=ps_cb[:, 0:1024]))
        vdrain()
        v_rsbc = cnt["v"]
        for g in range(8):
            wait("sync", "v")
            run("sync", "din", lambda E, g=g: E.dma_start(
                out=ffn_sb[:, :], in_=rhs[g * 128:(g + 1) * 128, :]), 16)
            wait("vector", "din")
            vop(lambda E, g=g: E.tensor_tensor(
                out=ffn_sb[:, :], in0=ffn_sb[:, :], in1=rr_bc[:, :], op=OP.mult))
            vop(lambda E, g=g: E.tensor_scalar(
                out=ffn_sb[:, :], in0=ffn_sb[:, :], scalar1=RND, scalar2=None,
                op0=OP.add))
            vop(lambda E, g=g: E.tensor_scalar(
                out=qrhs_sb[:, g, :], in0=ffn_sb[:, :], scalar1=-RND, scalar2=None,
                op0=OP.add))
            vdrain()
        run("vector", "v", lambda E: E.memset(state_sb[:, :], 0.0))
        run("vector", "v", lambda E: E.memset(eps_sb[:, :], EPS))
        run("vector", "v", lambda E: E.memset(one_sb[:, :], 1.0))
        run("vector", "v", lambda E: E.memset(rnd_sb[:, :], RND))
        run("vector", "v", lambda E: E.memset(nrnd_sb[:, :], -RND))
        v_setup = cnt["v"]


        v_after_scan = [v_neg, v_neg]
        t_xtb_use = [0, 0]
        v_bu_use = v_rsbc
        v_ym_use = v_rsbc
        g_yacc = [0, 0]
        s_dt = [0, 0, 0]
        v_u = [0, 0, 0]
        v_xp_l = [0, 0, 0]
        v_xp_last = 0
        t_xp_use = [0, 0, 0]

        ncc_so_far = 0
        def proj_pieces(b):
            par = b % 2
            xpar = b % 3
            dpar = b % 3
            st = {}
            pieces = []
            def mk_c(c):
                def piece():
                    nonlocal v_xp_last
                    sl = c % 2
                    if c == 0:
                        wait("gpsimd", "cc", ncc_so_far)
                        wait("tensor", "v", max(v_setup, v_xp_last))
                    wait("gpsimd", "t", t_xtb_use[sl])
                    run("gpsimd", "g", lambda E, c=c, sl=sl, b=b: E.dma_start(
                        out=xtb_sb[:, sl, :],
                        in_=xtf[c * 128:(c + 1) * 128, b * TBK:(b + 1) * TBK]), 16)
                    wait("tensor", "g")
                    for qq in range(2):
                        run("tensor", "t", lambda E, c=c, sl=sl, qq=qq: E.matmul(
                            out=ps_proj[:96, qq * 512:(qq + 1) * 512], lhsT=wxp_sb[:, c, :],
                            rhs=xtb_sb[:, sl, qq * 512:(qq + 1) * 512],
                            start=(c == 0), stop=(c == 7)))
                    for qq in range(2):
                        psx = ps_tr if qq == 0 else ps_f
                        run("tensor", "t", lambda E, c=c, sl=sl, qq=qq, psx=psx: E.matmul(
                            out=psx[:96, 0:512], lhsT=wxp_sb[:, c, :],
                            rhs=xtb_sb[:, sl, 1024 + qq * 512:1024 + (qq + 1) * 512],
                            start=(c == 0), stop=(c == 7)))
                    t_xtb_use[sl] = cnt["t"]
                    if c == 7:
                        st["t_xpmm"] = cnt["t"]
                return piece
            for c in range(8):
                pieces.append(mk_c(c))
            def piece_evac():
                nonlocal v_xp_last
                wait("vector", "t", max(st["t_xpmm"], t_xp_use[xpar]))
                vop(lambda E, xpar=xpar: E.tensor_copy(
                    out=xp_sb[:, xpar, 0:1024], in_=ps_proj[:96, :1024]))
                vop(lambda E, xpar=xpar: E.tensor_copy(
                    out=xp_sb[:, xpar, 1024:1536], in_=ps_tr[:96, 0:512]))
                vop(lambda E, xpar=xpar: E.tensor_copy(
                    out=xp_sb[:, xpar, 1536:2048], in_=ps_f[:96, 0:512]))
                vdrain()
                st["v_xp"] = cnt["v"]
                v_xp_last = cnt["v"]
            pieces.append(piece_evac)
            def mk_dt(half):
                def piece():
                    wait("tensor", "v", st["v_xp"])
                    wait("tensor", "g", setup_g)
                    for qq in range(2):
                        psx = ps_tr if qq == 0 else ps_f
                        run("tensor", "t", lambda E, xpar=xpar, half=half, qq=qq, psx=psx: E.matmul(
                            out=psx[:, 0:512], lhsT=wdt_sb[:, :],
                            rhs=xp_sb[0:64, xpar, half * 1024 + qq * 512:half * 1024 + (qq + 1) * 512],
                            start=True, stop=True))
                    t_dt = cnt["t"]
                    wait("scalar", "t", t_dt)
                    wait("scalar", "v", v_u[dpar])
                    for qq in range(2):
                        psx = ps_tr if qq == 0 else ps_f
                        c0 = half * 1024 + qq * 512
                        run("scalar", "s", lambda E, dpar=dpar, c0=c0, psx=psx: E.activation(
                            out=dt_sb[:, dpar, c0:c0 + 512], in_=psx[:, 0:512],
                            func=AF.Exp, bias=bdt_sb[:, :], scale=1.0))
                        run("scalar", "s", lambda E, dpar=dpar, c0=c0: E.activation(
                            out=dt_sb[:, dpar, c0:c0 + 512],
                            in_=dt_sb[:, dpar, c0:c0 + 512],
                            func=AF.Ln, bias=one_sb[:, :], scale=1.0))
                    wait("tensor", "s")
                    if half == 1:
                        s_dt[dpar] = cnt["s"]
                return piece
            pieces.append(mk_dt(0))
            pieces.append(mk_dt(1))
            def piece_u():
                wait("sync", "v", max(v_u))
                wait("sync", "cc", ncc_so_far)
                run("sync", "din", lambda E, b=b: E.dma_start(
                    out=xtd_sb[:, 0, :], in_=xtd[:, b * TBK:(b + 1) * TBK]), 16)
                din_xtd = cnt["din"]
                wait("vector", "s", s_dt[dpar])
                wait("vector", "din", din_xtd)
                run("vector", "v", lambda E, dpar=dpar: E.tensor_tensor(
                    out=u_sb[:, dpar, :], in0=dt_sb[:, dpar, :], in1=xtd_sb[:, 0, :],
                    op=OP.mult))
                v_u[dpar] = cnt["v"]
            pieces.append(piece_u)
            return pieces

        def emit_scan(b, feeder=None):
            nonlocal ncc_so_far, v_bu_use, v_ym_use
            par = b % 2
            xpar = b % 3
            dpar = b % 3
            q = b // 2
            wait("vector", "din", g_yacc[par])
            if b == 4:
                run("vector", "v", lambda E: E.memset(state_sb[:, :], 0.0))

            for n in range(N):
                ct = b * N + n
                abp = ct % 2
                hp = n % 2
                wait("scalar", "v", v_after_scan[abp])
                run("scalar", "s", lambda E, dpar=dpar, n=n, abp=abp: E.activation(
                    out=ab_sb[:, abp, :], in_=dt_sb[:, dpar, :], func=AF.Exp,
                    scale=a_sb[:, n:n + 1]))
                s_ab = cnt["s"]
                for half in range(2):
                    wait("tensor", "v", v_bu_use)
                    for qq in range(2):
                        run("tensor", "t", lambda E, par=par, n=n, half=half, qq=qq: E.matmul(
                            out=ps_bm[:, qq * 512:(qq + 1) * 512], lhsT=bcsel_sb[64:96, n, :],
                            rhs=xp_sb[64:96, xpar, half * 1024 + qq * 512:half * 1024 + (qq + 1) * 512],
                            start=True, stop=True))
                    t_bm = cnt["t"]
                    wait("vector", "t", t_bm)
                    run("vector", "v", lambda E, dpar=dpar, half=half: E.tensor_tensor(
                        out=bu_sb[:, half * 1024:(half + 1) * 1024],
                        in0=u_sb[:, dpar, half * 1024:(half + 1) * 1024],
                        in1=ps_bm[:, :1024], op=OP.mult))
                    v_bu_use = cnt["v"]
                wait("vector", "s", s_ab)
                if DEBUG and b == DBGBLK and n == 0:
                    wait("gpsimd", "v", cnt["v"])
                    wait("gpsimd", "s", cnt["s"])
                    run("gpsimd", "dout", lambda E, abp=abp: E.dma_start(out=d_ab[:, :], in_=ab_sb[:, abp, :]), 16)
                    run("gpsimd", "dout", lambda E: E.dma_start(out=d_bu[:, :], in_=bu_sb[:, :]), 16)
                    wait("vector", "dout")
                run("vector", "v", lambda E, n=n, abp=abp, hp=hp: E.tensor_tensor_scan(
                    out=h_sb[:, hp, :], data0=ab_sb[:, abp, :], data1=bu_sb[:, :],
                    initial=state_sb[:, n:n + 1], op0=OP.mult, op1=OP.add))
                v_after_scan[abp] = cnt["v"]
                if DEBUG and b == DBGBLK and n == 0:
                    vdrain()
                    wait("gpsimd", "v", cnt["v"])
                    run("gpsimd", "dout", lambda E, hp=hp: E.dma_start(out=d_h[:, :], in_=h_sb[:, hp, :]), 16)
                    wait("vector", "dout")
                if n >= 1:
                    run("vector", "v", lambda E, n=n, hp=hp: E.tensor_copy(
                        out=state_sb[:, n - 1:n], in_=h_sb[:, 1 - hp, TBK - 1:TBK]))
                for half in range(2):
                    wait("tensor", "v", v_ym_use)
                    for qq in range(2):
                        run("tensor", "t", lambda E, par=par, n=n, half=half, qq=qq: E.matmul(
                            out=ps_cb[:, qq * 512:(qq + 1) * 512], lhsT=bcsel_sb[64:96, 16 + n, :],
                            rhs=xp_sb[64:96, xpar, half * 1024 + qq * 512:half * 1024 + (qq + 1) * 512],
                            start=True, stop=True))
                    t_cb = cnt["t"]
                    wait("vector", "t", t_cb)
                    if n == 0:
                        run("vector", "v", lambda E, par=par, half=half, hp=hp: E.tensor_tensor(
                            out=yacc_sb[:, par, half * 1024:(half + 1) * 1024],
                            in0=h_sb[:, hp, half * 1024:(half + 1) * 1024],
                            in1=ps_cb[:, :1024], op=OP.mult))
                    else:
                        run("vector", "v", lambda E, half=half, hp=hp: E.tensor_tensor(
                            out=bu_sb[:, half * 1024:(half + 1) * 1024],
                            in0=h_sb[:, hp, half * 1024:(half + 1) * 1024],
                            in1=ps_cb[:, :1024], op=OP.mult))
                        run("vector", "v", lambda E, par=par, half=half: E.tensor_tensor(
                            out=yacc_sb[:, par, half * 1024:(half + 1) * 1024],
                            in0=yacc_sb[:, par, half * 1024:(half + 1) * 1024],
                            in1=bu_sb[:, half * 1024:(half + 1) * 1024], op=OP.add))
                    v_ym_use = cnt["v"]
                if feeder is not None and n >= 6:
                    p = next(feeder, None)
                    if p is not None:
                        p()
            t_xp_use[xpar] = cnt["t"]
            run("vector", "v", lambda E: E.tensor_copy(
                out=state_sb[:, N - 1:N], in_=h_sb[:, (N - 1) % 2, TBK - 1:TBK]))
            vdrain()
            v_yacc = cnt["v"]
            if DEBUG and b == DBGBLK:
                wait("gpsimd", "v", v_yacc)
                run("gpsimd", "dout", lambda E, par=par: E.dma_start(out=d_yacc[:, :], in_=yacc_sb[:, par, :]), 16)
                wait("vector", "dout")
            wait("sync", "v", v_yacc)
            wait("sync", "cc", ncc_so_far)
            run("sync", "din", lambda E, b=b, par=par: E.dma_start(
                out=ybin[b, :, :], in_=yacc_sb[:, par, :]), 16)
            g_yacc[par] = cnt["din"]

        def emit_ag(b):
            nonlocal ncc_so_far
            wait("gpsimd", "g", cnt["g"])
            wait("gpsimd", "din", cnt["din"])
            run("gpsimd", "cc", lambda E, b=b: E.collective_compute(
                "AllGather", OP.bypass, replica_groups=[list(range(NCORE))],
                ins=[ybin[b].opt()], outs=[ybout[b].opt()]))
            ncc_so_far = cnt["cc"]



        for p in proj_pieces(0):
            p()
        for p in proj_pieces(1):
            p()
        for b in range(NBK):
            feeder = iter(proj_pieces(b + 2)) if b + 2 < NBK else None
            emit_scan(b, feeder)
            if feeder is not None:
                for p in feeder:
                    p()
            emit_ag(b)
        wait("gpsimd", "cc", NBK)
        wait("sync", "cc", NBK)
        def ycopy(E, SEM, piece):
            E.reg_add(REG["s"], REG["rb"], piece * (OWN // 4))
            E.dma_start(out=yown[:, piece * (OWN // 4):(piece + 1) * (OWN // 4)],
                        in_=bass.AP(ybout_h, REG["s"], [[TBK, D], [1, OWN // 4]])
                        ).then_inc(SEM["g"], 16)
        emit("gpsimd", lambda E, SEM: ycopy(E, SEM, 0))
        cnt["g"] += 16
        g_yown_p = [cnt["g"]]
        for piece in range(1, 4):
            emit("gpsimd", lambda E, SEM, piece=piece: ycopy(E, SEM, piece))
            cnt["g"] += 16
            g_yown_p.append(cnt["g"])

        dout_after_tc = [0, 0]
        v_z1_done = [v_setup, v_setup]
        t_tr_done = [0, 0]
        g_ytc_l = [0, 0]
        din_x_l = [0, 0]
        s_sq1 = [0, 0]
        s_sq2 = [0, 0]

        def ffn_loads(tc):
            fp = tc % 2
            wait("gpsimd", "t", t_tr_done[fp])
            wait("gpsimd", "g", g_yown_p[min(tc // 4, 3)])
            run("gpsimd", "g", lambda E, tc=tc, fp=fp: E.dma_start(
                out=ytc_sb[:, fp, :].rearrange("p (s c) -> p s c", s=8),
                in_=yown.rearrange("(s p) t -> p s t", p=128)[:, :, tc * 128:(tc + 1) * 128]), 16)
            g_ytc_l[fp] = cnt["g"]
            wait("sync", "v", v_z1_done[fp])
            run("sync", "din", lambda E, tc=tc, fp=fp: E.dma_start(
                out=xstg_sb[:, fp, :], in_=xown[tc * 128:(tc + 1) * 128, :]), 16)
            din_x_l[fp] = cnt["din"]

        def stage1(tc):
            fp = tc % 2
            z1 = ab_sb[:, fp, 0:1024]
            wait("tensor", "g", g_ytc_l[fp])
            wait("tensor", "v", v_z1_done[1 - fp])
            for s8 in range(8):
                run("tensor", "t", lambda E, s8=s8, fp=fp: E.transpose(
                    out=ps_proj[:, s8 * 128:(s8 + 1) * 128],
                    in_=ytc_sb[:, fp, s8 * 128:(s8 + 1) * 128], identity=ident_sb[:, :]))
            t_ytr = cnt["t"]
            t_tr_done[fp] = t_ytr
            wait("vector", "t", t_ytr)
            wait("vector", "din", din_x_l[fp])
            run("vector", "v", lambda E, fp=fp, z1=z1: E.tensor_tensor(
                out=z1, in0=xstg_sb[:, fp, :], in1=ps_proj[:, :], op=OP.add))
            vdrain()
            v_z1_done[fp] = cnt["v"]
            vop(lambda E, fp=fp, z1=z1: E.bn_stats(out=bnst_sb[:, fp, 0:6], in_=z1[:, 0:512]))
            vop(lambda E, fp=fp, z1=z1: E.bn_stats(out=bnst_sb[:, fp, 6:12], in_=z1[:, 512:1024]))
            vdrain()
            vop(lambda E, fp=fp: E.bn_aggr(out=cols_sb[:, fp, 2:4], in_=bnst_sb[:, fp, :]))
            vdrain()
            v_var = cnt["v"]
            wait("scalar", "v", v_var)
            run("scalar", "s", lambda E, fp=fp: E.activation(
                out=cols_sb[:, fp, 5:6], in_=cols_sb[:, fp, 3:4], func=AF.Sqrt,
                bias=eps_sb[:, :], scale=1.0))
            s_sq1[fp] = cnt["s"]

        def stage2(tc):
            fp = tc % 2
            z1 = ab_sb[:, fp, 0:1024]
            z2 = ab_sb[:, fp, 1024:2048]
            ql = h_sb[:, fp, 0:1024]
            wait("vector", "s", s_sq1[fp])
            vop(lambda E, fp=fp: E.reciprocal(out=cols_sb[:, fp, 6:7], in_=cols_sb[:, fp, 5:6]))
            vdrain()
            vop(lambda E, fp=fp: E.tensor_scalar(
                out=cols_sb[:, fp, 16:17], in0=cols_sb[:, fp, 2:3],
                scalar1=cols_sb[:, fp, 6:7], scalar2=-1.0, op0=OP.mult, op1=OP.mult))
            vdrain()
            v_nb = cnt["v"]
            wait("scalar", "v", v_nb)
            run("scalar", "s", lambda E, fp=fp, z1=z1: E.activation(
                out=z1, in_=z1, func=AF.Identity,
                bias=cols_sb[:, fp, 16:17], scale=cols_sb[:, fp, 6:7]))
            s_z1n = cnt["s"]
            wait("vector", "s", s_z1n)
            vop(lambda E, fp=fp, z1=z1: E.tensor_reduce(
                out=cols_sb[:, fp, 7:8], in_=z1, axis=AX.X, op=OP.max,
                apply_absolute_value=True))
            vdrain()
            vop(lambda E, fp=fp: E.tensor_scalar(
                out=cols_sb[:, fp, 8:9], in0=cols_sb[:, fp, 7:8], scalar1=1e-12,
                scalar2=1.0 / 127.0, op0=OP.max, op1=OP.mult))
            vdrain()
            vop(lambda E, fp=fp: E.reciprocal(out=cols_sb[:, fp, 9:10], in_=cols_sb[:, fp, 8:9]))
            vdrain()
            v_rr = cnt["v"]
            wait("scalar", "v", v_rr)
            run("scalar", "s", lambda E, fp=fp, z1=z1, ql=ql: E.activation(
                out=ql, in_=z1, func=AF.Identity,
                bias=rnd_sb[:, :], scale=cols_sb[:, fp, 9:10]))
            run("scalar", "s", lambda E, ql=ql: E.activation(
                out=ql, in_=ql, func=AF.Identity,
                bias=nrnd_sb[:, :], scale=1.0))
            s_ql = cnt["s"]
            wait("tensor", "s", s_ql)
            wait("tensor", "v", v_rr)
            for g in range(4):
                run("tensor", "t", lambda E, g=g, ql=ql: E.transpose(
                    out=ps_tr[:, g * 128:(g + 1) * 128],
                    in_=ql[:, g * 128:(g + 1) * 128], identity=ident_sb[:, :]))
            t_trA = cnt["t"]
            for g in range(4, 8):
                run("tensor", "t", lambda E, g=g, ql=ql: E.transpose(
                    out=ps_f[:, (g - 4) * 128:(g - 3) * 128],
                    in_=ql[:, g * 128:(g + 1) * 128], identity=ident_sb[:, :]))
            t_trB = cnt["t"]
            wait("vector", "t", t_trA)
            run("vector", "v", lambda E: E.tensor_copy(
                out=qlT_sb[:, 0:4, :], in_=ps_tr[:, 0:512]))
            wait("vector", "t", t_trB)
            run("vector", "v", lambda E: E.tensor_copy(
                out=qlT_sb[:, 4:8, :], in_=ps_f[:, 0:512]))
            vdrain()
            v_qlT = cnt["v"]
            wait("tensor", "v", v_qlT)
            for jb in range(2):
                psf = ps_f if jb == 0 else ps_tr
                for c in range(8):
                    run("tensor", "t", lambda E, c=c, jb=jb, psf=psf: E.matmul(
                        out=psf[:, 0:512], lhsT=qlT_sb[:, c, :],
                        rhs=qrhs_sb[:, c, jb * 512:(jb + 1) * 512],
                        start=(c == 0), stop=(c == 7)))
                t_ffn = cnt["t"]
                wait("vector", "t", t_ffn)
                run("vector", "v", lambda E, jb=jb, fp=fp, psf=psf: E.scalar_tensor_tensor(
                    out=ffn_sb[:, jb * 512:(jb + 1) * 512], in0=psf[:, 0:512],
                    scalar=cols_sb[:, fp, 8:9], in1=rs_bc[:, jb * 512:(jb + 1) * 512],
                    op0=OP.mult, op1=OP.mult))
            v_stt2 = cnt["v"]
            run("vector", "v", lambda E, z1=z1, z2=z2: E.tensor_tensor(
                out=z2, in0=z1, in1=ffn_sb[:, :], op=OP.add))
            vop(lambda E, fp=fp, z2=z2: E.bn_stats(out=bnst_sb[:, fp, 0:6], in_=z2[:, 0:512]))
            vop(lambda E, fp=fp, z2=z2: E.bn_stats(out=bnst_sb[:, fp, 6:12], in_=z2[:, 512:1024]))
            vdrain()
            vop(lambda E, fp=fp: E.bn_aggr(out=cols_sb[:, fp, 12:14], in_=bnst_sb[:, fp, :]))
            vdrain()
            v_var2 = cnt["v"]
            wait("scalar", "v", v_var2)
            run("scalar", "s", lambda E, fp=fp: E.activation(
                out=cols_sb[:, fp, 15:16], in_=cols_sb[:, fp, 13:14], func=AF.Sqrt,
                bias=eps_sb[:, :], scale=1.0))
            s_sq2[fp] = cnt["s"]

        def stage3(tc):
            fp = tc % 2
            z2 = ab_sb[:, fp, 1024:2048]
            wait("vector", "s", s_sq2[fp])
            vop(lambda E, fp=fp: E.reciprocal(out=cols_sb[:, fp, 14:15], in_=cols_sb[:, fp, 15:16]))
            vdrain()
            vop(lambda E, fp=fp: E.tensor_scalar(
                out=cols_sb[:, fp, 17:18], in0=cols_sb[:, fp, 12:13],
                scalar1=cols_sb[:, fp, 14:15], scalar2=-1.0, op0=OP.mult, op1=OP.mult))
            vdrain()
            v_nb2 = cnt["v"]
            wait("scalar", "v", v_nb2)
            wait("scalar", "dout", dout_after_tc[fp])
            run("scalar", "s", lambda E, fp=fp, z2=z2: E.activation(
                out=u_sb[:, fp, 0:D], in_=z2, func=AF.Identity,
                bias=cols_sb[:, fp, 17:18], scale=cols_sb[:, fp, 14:15]))
            s_out = cnt["s"]
            wait("gpsimd", "s", s_out)
            run("gpsimd", "dout", lambda E, tc=tc, fp=fp: E.dma_start(
                out=out[tc * 128:(tc + 1) * 128, :], in_=u_sb[:, fp, 0:D]), 16)
            dout_after_tc[fp] = cnt["dout"]

        ffn_loads(0)
        ffn_loads(1)
        stage1(0)
        for tc in range(16):
            if tc + 2 < 16:
                ffn_loads(tc + 2)
            if tc + 1 < 16:
                stage1(tc + 1)
            stage2(tc)
            if tc >= 1:
                stage3(tc - 1)
        stage3(15)
        wait("gpsimd", "dout", cnt["dout"])

        @blk_.sync
        def _(E):
            for f in ops["sync"]:
                f(E, SEMH)

        @blk_.tensor
        def _(E):
            for f in ops["tensor"]:
                f(E, SEMH)

        @blk_.scalar
        def _(E):
            for f in ops["scalar"]:
                f(E, SEMH)

        @blk_.vector
        def _(E):
            for f in ops["vector"]:
                f(E, SEMH)

        @blk_.gpsimd
        def _(E):
            for f in ops["gpsimd"]:
                f(E, SEMH)

    return nc


def kernel(L0_attn_x, L0_attn_W_xproj, L0_attn_W_dt, L0_attn_b_dt, L0_attn_A_log,
           L0_ffn_rhs):
    x = np.asarray(L0_attn_x, dtype=np.float32).reshape(T, D)
    wxp0 = np.asarray(L0_attn_W_xproj, dtype=np.float32)
    wxp = np.ascontiguousarray(
        np.concatenate([wxp0[:, 32:96], wxp0[:, 0:16], wxp0[:, 16:32]], axis=1))
    wdt = np.asarray(L0_attn_W_dt, dtype=np.float32)
    bdt = np.asarray(L0_attn_b_dt, dtype=np.float32)
    alog = np.asarray(L0_attn_A_log, dtype=np.float32)
    rhs = np.ascontiguousarray(np.asarray(L0_ffn_rhs, dtype=np.float32))
    rhsT = np.ascontiguousarray(rhs.T)
    xtf = np.ascontiguousarray(x.T)
    ident = np.eye(128, dtype=np.float32)
    bcsel = np.zeros((32, 32, 128), dtype=np.float32)
    bcsel[np.arange(32), np.arange(32), :] = 1.0

    nc = build_program()
    in_maps = []
    for k in range(NCORE):
        dsl = slice(k * 128, (k + 1) * 128)
        in_maps.append({
            "xtf": xtf,
            "xtd": np.ascontiguousarray(xtf[dsl, :]),
            "xown": np.ascontiguousarray(x[k * OWN:(k + 1) * OWN, :]),
            "wxp": wxp,
            "wdtd": np.ascontiguousarray(wdt[:, dsl]),
            "bdtd": np.ascontiguousarray(bdt[dsl].reshape(128, 1)),
            "alogd": np.ascontiguousarray(alog.T[dsl, :]),
            "rhs": rhs, "rhsT": rhsT, "ident": ident, "bcsel": bcsel,
        })
    res = run_bass_kernel_spmd(nc, in_maps, core_ids=list(range(NCORE)))
    shards = [res.results[c]["out"] for c in range(NCORE)]
    return np.concatenate(shards, axis=0).reshape(B, S, D).astype(np.float32)


# version 27
# speedup vs baseline: 1.1617x; 1.1617x over previous
import contextlib

import numpy as np
import concourse.bass as bass
import concourse.mybir as mybir
from concourse.bass_types import DynSlice
from concourse.bass_utils import run_bass_kernel_spmd

F32 = mybir.dt.float32
BF16 = mybir.dt.bfloat16
AF = mybir.ActivationFunctionType
OP = mybir.AluOpType
AX = mybir.AxisListType

B, S, D, N, RK = 2, 8192, 1024, 16, 64
NCORE = 8
T = B * S
OWN = T // NCORE
TBK = 2048
NBK = T // TBK
QT = 4096
RND = 12582912.0
EPS = 1e-5
ENGS = ("sync", "tensor", "scalar", "vector", "gpsimd")
DEBUG = False
DBGBLK = 0


def build_program():
    nc = bass.Bass("TRN2", target_bir_lowering=False)

    xtf = nc.dram_tensor("xtf", [D, T], F32, kind="ExternalInput").ap()
    xtd = nc.dram_tensor("xtd", [128, T], F32, kind="ExternalInput").ap()
    xown = nc.dram_tensor("xown", [OWN, D], F32, kind="ExternalInput").ap()
    wxp = nc.dram_tensor("wxp", [D, 96], F32, kind="ExternalInput").ap()
    wdtd = nc.dram_tensor("wdtd", [RK, 128], F32, kind="ExternalInput").ap()
    bdtd = nc.dram_tensor("bdtd", [128, 1], F32, kind="ExternalInput").ap()
    alogd = nc.dram_tensor("alogd", [128, N], F32, kind="ExternalInput").ap()
    rhs = nc.dram_tensor("rhs", [D, D], F32, kind="ExternalInput").ap()
    rhsT = nc.dram_tensor("rhsT", [D, D], F32, kind="ExternalInput").ap()
    ident = nc.dram_tensor("ident", [128, 128], F32, kind="ExternalInput").ap()
    bcsel = nc.dram_tensor("bcsel", [32, 32, 128], F32, kind="ExternalInput").ap()
    out = nc.dram_tensor("out", [OWN, D], F32, kind="ExternalOutput").ap()
    if DEBUG:
        d_xp = nc.dram_tensor("d_xp", [96, TBK], F32, kind="ExternalOutput").ap()
        d_dt = nc.dram_tensor("d_dt", [128, TBK], F32, kind="ExternalOutput").ap()
        d_u = nc.dram_tensor("d_u", [128, TBK], F32, kind="ExternalOutput").ap()
        d_ab = nc.dram_tensor("d_ab", [128, TBK], F32, kind="ExternalOutput").ap()
        d_bu = nc.dram_tensor("d_bu", [128, TBK], F32, kind="ExternalOutput").ap()
        d_h = nc.dram_tensor("d_h", [128, TBK], F32, kind="ExternalOutput").ap()
        d_yacc = nc.dram_tensor("d_yacc", [128, TBK], F32, kind="ExternalOutput").ap()
        d_ytc = nc.dram_tensor("d_ytc", [128, D], F32, kind="ExternalOutput").ap()
        d_z1 = nc.dram_tensor("d_z1", [128, D], F32, kind="ExternalOutput").ap()
        d_z1n = nc.dram_tensor("d_z1n", [128, D], F32, kind="ExternalOutput").ap()
        d_ql = nc.dram_tensor("d_ql", [128, D], F32, kind="ExternalOutput").ap()
        d_ffn = nc.dram_tensor("d_ffn", [128, D], F32, kind="ExternalOutput").ap()
        d_cols = nc.dram_tensor("d_cols", [128, 16], F32, kind="ExternalOutput").ap()
        d_qrhs = nc.dram_tensor("d_qrhs", [128, 8, D], F32, kind="ExternalOutput").ap()
        d_rsbc = nc.dram_tensor("d_rsbc", [128, D], F32, kind="ExternalOutput").ap()
    ybin_h = nc.dram_tensor("ybin", [NBK, 128, TBK], F32)
    ybout_h = nc.dram_tensor("ybout", [NBK, D, TBK], F32)
    yown_h = nc.dram_tensor("yown", [D, OWN], F32)
    ybin = ybin_h.ap()
    ybout = ybout_h.ap()
    yown = yown_h.ap()

    ops = {e: [] for e in ENGS}
    cnt = {c: 0 for c in ("din", "g", "cc", "t", "s", "v", "dout")}

    def emit(e, fn):
        ops[e].append(fn)

    def wait(e, c, val=None):
        v = cnt[c] if val is None else val
        if v > 0:
            emit(e, lambda E, SEM, c=c, v=v: E.wait_ge(SEM[c], v))

    def run(e, c, fn, n=1):
        emit(e, lambda E, SEM, fn=fn, c=c, n=n: fn(E).then_inc(SEM[c], n))
        cnt[c] += n

    def vop(fn):
        emit("vector", lambda E, SEM, fn=fn: fn(E))

    def vdrain():
        emit("vector", lambda E, SEM: E.drain(fusable=False))
        emit("vector", lambda E, SEM: E.sem_inc(SEM["v"], 1))
        cnt["v"] += 1

    with contextlib.ExitStack() as ctx:
        ent = ctx.enter_context
        wxp_sb = ent(nc.sbuf_tensor("wxp_sb", [128, 8, 96], BF16))
        wdt_sb = ent(nc.sbuf_tensor("wdt_sb", [RK, 128], BF16))
        bdt_sb = ent(nc.sbuf_tensor("bdt_sb", [128, 1], F32))
        alog_sb = ent(nc.sbuf_tensor("alog_sb", [128, N], F32))
        a_sb = ent(nc.sbuf_tensor("a_sb", [128, N], F32))
        ident_sb = ent(nc.sbuf_tensor("ident_sb", [128, 128], F32))
        bcsel_sb = ent(nc.sbuf_tensor("bcsel_sb", [96, 32, 128], BF16))
        bcself_sb = ent(nc.sbuf_tensor("bcself_sb", [72, 8, 128], F32))
        rs_col = ent(nc.sbuf_tensor("rs_col", [128, 8], F32))
        rr_col = ent(nc.sbuf_tensor("rr_col", [128, 8], F32))
        rsrow_tmp = ent(nc.sbuf_tensor("rsrow_tmp", [8, 2, 128], F32))
        rsrow_sb = ent(nc.sbuf_tensor("rsrow_sb", [72, 2, 128], F32))
        rs_bc = ent(nc.sbuf_tensor("rs_bc", [128, D], F32))
        rr_bc = ent(nc.sbuf_tensor("rr_bc", [128, D], F32))
        qrhs_sb = ent(nc.sbuf_tensor("qrhs_sb", [128, 8, D], BF16))
        xtb_sb = ent(nc.sbuf_tensor("xtb_sb", [128, 2, TBK], BF16))
        xtd_sb = ent(nc.sbuf_tensor("xtd_sb", [128, 1, TBK], F32))
        xp_sb = ent(nc.sbuf_tensor("xp_sb", [96, 3, TBK], BF16))
        dt_sb = ent(nc.sbuf_tensor("dt_sb", [128, 3, TBK], F32))
        u_sb = ent(nc.sbuf_tensor("u_sb", [128, 3, TBK], F32))
        ab_sb = ent(nc.sbuf_tensor("ab_sb", [128, 2, TBK], F32))
        bu_sb = ent(nc.sbuf_tensor("bu_sb", [128, TBK], F32))
        h_sb = ent(nc.sbuf_tensor("h_sb", [128, 2, TBK], F32))
        yacc_sb = ent(nc.sbuf_tensor("yacc_sb", [128, 2, TBK], F32))
        state_sb = ent(nc.sbuf_tensor("state_sb", [128, N], F32))
        ytc_sb = ent(nc.sbuf_tensor("ytc_sb", [128, 2, D], F32))
        xstg_sb = ent(nc.sbuf_tensor("xstg_sb", [128, 2, D], F32))
        z1_sb = ent(nc.sbuf_tensor("z1_sb", [128, D], F32))
        z2_sb = ent(nc.sbuf_tensor("z2_sb", [128, D], F32))
        ql_sb = ent(nc.sbuf_tensor("ql_sb", [128, D], F32))
        qlT_sb = ent(nc.sbuf_tensor("qlT_sb", [128, 8, 128], BF16))
        ffn_sb = ent(nc.sbuf_tensor("ffn_sb", [128, D], F32))
        cols_sb = ent(nc.sbuf_tensor("cols_sb", [128, 2, 20], F32))
        bnst_sb = ent(nc.sbuf_tensor("bnst_sb", [128, 2, 12], F32))
        eps_sb = ent(nc.sbuf_tensor("eps_sb", [128, 1], F32))
        one_sb = ent(nc.sbuf_tensor("one_sb", [128, 1], F32))
        rnd_sb = ent(nc.sbuf_tensor("rnd_sb", [128, 1], F32))
        nrnd_sb = ent(nc.sbuf_tensor("nrnd_sb", [128, 1], F32))
        ps_proj = ent(nc.psum_tensor("ps_proj", [128, 1024], F32))
        ps_bm = ent(nc.psum_tensor("ps_bm", [128, 1024], F32))
        ps_cb = ent(nc.psum_tensor("ps_cb", [128, 1024], F32))
        ps_tr = ent(nc.psum_tensor("ps_tr", [128, 512], F32))
        ps_f = ent(nc.psum_tensor("ps_f", [128, 512], F32))
        sm_din = ent(nc.semaphore("sm_din"))
        sm_g = ent(nc.semaphore("sm_g"))
        sm_cc = ent(nc.semaphore("sm_cc"))
        sm_t = ent(nc.semaphore("sm_t"))
        sm_s = ent(nc.semaphore("sm_s"))
        sm_v = ent(nc.semaphore("sm_v"))
        sm_dout = ent(nc.semaphore("sm_dout"))
        blk_ = ent(nc.Block())
        SEMH = {"din": sm_din, "g": sm_g, "cc": sm_cc, "t": sm_t, "s": sm_s,
                "v": sm_v, "dout": sm_dout}

        REG = {}

        def initreg(E, SEM):
            pid = E.partition_id()
            rbv = E.compute_val(pid * (D * TBK))
            REG["rb"] = rbv
            REG["s"] = E.alloc_register("yoffreg")
        emit("gpsimd", initreg)
        run("gpsimd", "g", lambda E: E.dma_start(
            out=wxp_sb[:, :, :], in_=wxp.rearrange("(c p) f -> p c f", p=128)), 16)
        run("gpsimd", "g", lambda E: E.dma_start(out=wdt_sb[:, :], in_=wdtd[:, :]), 16)
        run("gpsimd", "g", lambda E: E.dma_start(
            out=bcsel_sb[64:96, :, :], in_=bcsel[:, :, :]), 16)
        run("sync", "din", lambda E: E.dma_start(out=bdt_sb[:, :], in_=bdtd[:, :]), 16)
        run("sync", "din", lambda E: E.dma_start(out=alog_sb[:, :], in_=alogd[:, :]), 16)
        run("sync", "din", lambda E: E.dma_start(out=ident_sb[:, :], in_=ident[:, :]), 16)
        run("sync", "din", lambda E: E.dma_start(
            out=bcself_sb[64:72, :, :], in_=bcsel[0:8, 0:8, :]), 16)
        setup_din = cnt["din"]
        setup_g = cnt["g"]

        wait("scalar", "din", setup_din)
        run("scalar", "s", lambda E: E.activation(
            out=a_sb[:, :], in_=alog_sb[:, :], func=AF.Exp))
        wait("vector", "s")
        run("vector", "v", lambda E: E.tensor_scalar(
            out=a_sb[:, :], in0=a_sb[:, :], scalar1=-1.0, scalar2=None, op0=OP.mult))
        vdrain()
        v_neg = cnt["v"]

        for g in range(8):
            wait("sync", "v")
            run("sync", "din", lambda E, g=g: E.dma_start(
                out=ffn_sb[:, :], in_=rhsT[g * 128:(g + 1) * 128, :]), 16)
            wait("vector", "din")
            vop(lambda E, g=g: E.tensor_reduce(
                out=rs_col[:, g:g + 1], in_=ffn_sb[:, :], axis=AX.X, op=OP.max,
                apply_absolute_value=True))
            vdrain()
        vop(lambda E: E.tensor_scalar(
            out=rs_col[:, :], in0=rs_col[:, :], scalar1=1e-12, scalar2=1.0 / 127.0,
            op0=OP.max, op1=OP.mult))
        vdrain()
        vop(lambda E: E.reciprocal(out=rr_col[:, :], in_=rs_col[:, :]))
        vdrain()
        v_rs_done = cnt["v"]
        wait("tensor", "din", setup_din)
        wait("tensor", "v", v_rs_done)
        run("tensor", "t", lambda E: E.transpose(
            out=ps_tr[:8, 0:128], in_=rs_col[:, :], identity=ident_sb[:, :]))
        run("tensor", "t", lambda E: E.transpose(
            out=ps_tr[:8, 128:256], in_=rr_col[:, :], identity=ident_sb[:, :]))
        t_rows = cnt["t"]
        wait("vector", "t", t_rows)
        vop(lambda E: E.tensor_copy(out=rsrow_tmp[:, 0, :], in_=ps_tr[:8, 0:128]))
        vop(lambda E: E.tensor_copy(out=rsrow_tmp[:, 1, :], in_=ps_tr[:8, 128:256]))
        vdrain()
        v_rows = cnt["v"]
        wait("sync", "v", v_rows)
        run("sync", "din", lambda E: E.dma_start(
            out=rsrow_sb[64:72, :, :], in_=rsrow_tmp[:, :, :]), 16)
        rows_din = cnt["din"]
        wait("tensor", "din", rows_din)
        for g in range(8):
            run("tensor", "t", lambda E, g=g: E.matmul(
                out=ps_bm[:, g * 128:(g + 1) * 128] if g < 8 else None,
                lhsT=bcself_sb[64:72, g, :],
                rhs=rsrow_sb[64:72, 0, :], start=True, stop=True))
        for g in range(8):
            run("tensor", "t", lambda E, g=g: E.matmul(
                out=ps_cb[:, g * 128:(g + 1) * 128], lhsT=bcself_sb[64:72, g, :],
                rhs=rsrow_sb[64:72, 1, :], start=True, stop=True))
        t_bc = cnt["t"]
        wait("vector", "t", t_bc)
        vop(lambda E: E.tensor_copy(out=rs_bc[:, 0:1024], in_=ps_bm[:, 0:1024]))
        vop(lambda E: E.tensor_copy(out=rr_bc[:, 0:1024], in_=ps_cb[:, 0:1024]))
        vdrain()
        v_rsbc = cnt["v"]
        for g in range(8):
            wait("sync", "v")
            run("sync", "din", lambda E, g=g: E.dma_start(
                out=ffn_sb[:, :], in_=rhs[g * 128:(g + 1) * 128, :]), 16)
            wait("vector", "din")
            vop(lambda E, g=g: E.tensor_tensor(
                out=ffn_sb[:, :], in0=ffn_sb[:, :], in1=rr_bc[:, :], op=OP.mult))
            vop(lambda E, g=g: E.tensor_scalar(
                out=ffn_sb[:, :], in0=ffn_sb[:, :], scalar1=RND, scalar2=None,
                op0=OP.add))
            vop(lambda E, g=g: E.tensor_scalar(
                out=qrhs_sb[:, g, :], in0=ffn_sb[:, :], scalar1=-RND, scalar2=None,
                op0=OP.add))
            vdrain()
        run("vector", "v", lambda E: E.memset(state_sb[:, :], 0.0))
        run("vector", "v", lambda E: E.memset(eps_sb[:, :], EPS))
        run("vector", "v", lambda E: E.memset(one_sb[:, :], 1.0))
        run("vector", "v", lambda E: E.memset(rnd_sb[:, :], RND))
        run("vector", "v", lambda E: E.memset(nrnd_sb[:, :], -RND))
        v_setup = cnt["v"]


        v_after_scan = [v_neg, v_neg]
        t_xtb_use = [0, 0]
        v_bu_use = v_rsbc
        v_ym_use = v_rsbc
        g_yacc = [0, 0]
        s_dt = [0, 0, 0]
        v_u = [0, 0, 0]
        v_xp_l = [0, 0, 0]
        v_xp_last = 0
        t_xp_use = [0, 0, 0]

        ncc_so_far = 0
        def proj_pieces(b):
            par = b % 2
            xpar = b % 3
            dpar = b % 3
            st = {}
            pieces = []
            def mk_c(c):
                def piece():
                    nonlocal v_xp_last
                    sl = c % 2
                    if c == 0:
                        wait("gpsimd", "cc", ncc_so_far)
                        wait("tensor", "v", max(v_setup, v_xp_last))
                    wait("gpsimd", "t", t_xtb_use[sl])
                    run("gpsimd", "g", lambda E, c=c, sl=sl, b=b: E.dma_start(
                        out=xtb_sb[:, sl, :],
                        in_=xtf[c * 128:(c + 1) * 128, b * TBK:(b + 1) * TBK]), 16)
                    wait("tensor", "g")
                    for qq in range(2):
                        run("tensor", "t", lambda E, c=c, sl=sl, qq=qq: E.matmul(
                            out=ps_proj[:96, qq * 512:(qq + 1) * 512], lhsT=wxp_sb[:, c, :],
                            rhs=xtb_sb[:, sl, qq * 512:(qq + 1) * 512],
                            start=(c == 0), stop=(c == 7)))
                    for qq in range(2):
                        psx = ps_tr if qq == 0 else ps_f
                        run("tensor", "t", lambda E, c=c, sl=sl, qq=qq, psx=psx: E.matmul(
                            out=psx[:96, 0:512], lhsT=wxp_sb[:, c, :],
                            rhs=xtb_sb[:, sl, 1024 + qq * 512:1024 + (qq + 1) * 512],
                            start=(c == 0), stop=(c == 7)))
                    t_xtb_use[sl] = cnt["t"]
                    if c == 7:
                        st["t_xpmm"] = cnt["t"]
                return piece
            for c in range(8):
                pieces.append(mk_c(c))
            def piece_evac():
                nonlocal v_xp_last
                wait("vector", "t", max(st["t_xpmm"], t_xp_use[xpar]))
                vop(lambda E, xpar=xpar: E.tensor_copy(
                    out=xp_sb[:, xpar, 0:1024], in_=ps_proj[:96, :1024]))
                vop(lambda E, xpar=xpar: E.tensor_copy(
                    out=xp_sb[:, xpar, 1024:1536], in_=ps_tr[:96, 0:512]))
                vop(lambda E, xpar=xpar: E.tensor_copy(
                    out=xp_sb[:, xpar, 1536:2048], in_=ps_f[:96, 0:512]))
                vdrain()
                st["v_xp"] = cnt["v"]
                v_xp_last = cnt["v"]
            pieces.append(piece_evac)
            def mk_dt(half):
                def piece():
                    wait("tensor", "v", st["v_xp"])
                    wait("tensor", "g", setup_g)
                    for qq in range(2):
                        psx = ps_tr if qq == 0 else ps_f
                        run("tensor", "t", lambda E, xpar=xpar, half=half, qq=qq, psx=psx: E.matmul(
                            out=psx[:, 0:512], lhsT=wdt_sb[:, :],
                            rhs=xp_sb[0:64, xpar, half * 1024 + qq * 512:half * 1024 + (qq + 1) * 512],
                            start=True, stop=True))
                    t_dt = cnt["t"]
                    wait("scalar", "t", t_dt)
                    wait("scalar", "v", v_u[dpar])
                    for qq in range(2):
                        psx = ps_tr if qq == 0 else ps_f
                        c0 = half * 1024 + qq * 512
                        run("scalar", "s", lambda E, dpar=dpar, c0=c0, psx=psx: E.activation(
                            out=dt_sb[:, dpar, c0:c0 + 512], in_=psx[:, 0:512],
                            func=AF.Exp, bias=bdt_sb[:, :], scale=1.0))
                        run("scalar", "s", lambda E, dpar=dpar, c0=c0: E.activation(
                            out=dt_sb[:, dpar, c0:c0 + 512],
                            in_=dt_sb[:, dpar, c0:c0 + 512],
                            func=AF.Ln, bias=one_sb[:, :], scale=1.0))
                    wait("tensor", "s")
                    if half == 1:
                        s_dt[dpar] = cnt["s"]
                return piece
            pieces.append(mk_dt(0))
            pieces.append(mk_dt(1))
            def piece_u():
                wait("sync", "v", max(v_u))
                wait("sync", "cc", ncc_so_far)
                run("sync", "din", lambda E, b=b: E.dma_start(
                    out=xtd_sb[:, 0, :], in_=xtd[:, b * TBK:(b + 1) * TBK]), 16)
                din_xtd = cnt["din"]
                wait("vector", "s", s_dt[dpar])
                wait("vector", "din", din_xtd)
                run("vector", "v", lambda E, dpar=dpar: E.tensor_tensor(
                    out=u_sb[:, dpar, :], in0=dt_sb[:, dpar, :], in1=xtd_sb[:, 0, :],
                    op=OP.mult))
                v_u[dpar] = cnt["v"]
            pieces.append(piece_u)
            return pieces

        def emit_scan(b, feeder=None):
            nonlocal ncc_so_far, v_bu_use, v_ym_use
            par = b % 2
            xpar = b % 3
            dpar = b % 3
            q = b // 2
            wait("vector", "din", g_yacc[par])
            if b == 4:
                run("vector", "v", lambda E: E.memset(state_sb[:, :], 0.0))

            for n in range(N):
                ct = b * N + n
                abp = ct % 2
                hp = n % 2
                wait("scalar", "v", v_after_scan[abp])
                run("scalar", "s", lambda E, dpar=dpar, n=n, abp=abp: E.activation(
                    out=ab_sb[:, abp, :], in_=dt_sb[:, dpar, :], func=AF.Exp,
                    scale=a_sb[:, n:n + 1]))
                s_ab = cnt["s"]
                for half in range(2):
                    wait("tensor", "v", v_bu_use)
                    for qq in range(2):
                        run("tensor", "t", lambda E, par=par, n=n, half=half, qq=qq: E.matmul(
                            out=ps_bm[:, qq * 512:(qq + 1) * 512], lhsT=bcsel_sb[64:96, n, :],
                            rhs=xp_sb[64:96, xpar, half * 1024 + qq * 512:half * 1024 + (qq + 1) * 512],
                            start=True, stop=True))
                    t_bm = cnt["t"]
                    wait("vector", "t", t_bm)
                    run("vector", "v", lambda E, dpar=dpar, half=half: E.tensor_tensor(
                        out=bu_sb[:, half * 1024:(half + 1) * 1024],
                        in0=u_sb[:, dpar, half * 1024:(half + 1) * 1024],
                        in1=ps_bm[:, :1024], op=OP.mult))
                    v_bu_use = cnt["v"]
                wait("vector", "s", s_ab)
                if DEBUG and b == DBGBLK and n == 0:
                    wait("gpsimd", "v", cnt["v"])
                    wait("gpsimd", "s", cnt["s"])
                    run("gpsimd", "dout", lambda E, abp=abp: E.dma_start(out=d_ab[:, :], in_=ab_sb[:, abp, :]), 16)
                    run("gpsimd", "dout", lambda E: E.dma_start(out=d_bu[:, :], in_=bu_sb[:, :]), 16)
                    wait("vector", "dout")
                run("vector", "v", lambda E, n=n, abp=abp, hp=hp: E.tensor_tensor_scan(
                    out=h_sb[:, hp, :], data0=ab_sb[:, abp, :], data1=bu_sb[:, :],
                    initial=state_sb[:, n:n + 1], op0=OP.mult, op1=OP.add))
                v_after_scan[abp] = cnt["v"]
                if DEBUG and b == DBGBLK and n == 0:
                    vdrain()
                    wait("gpsimd", "v", cnt["v"])
                    run("gpsimd", "dout", lambda E, hp=hp: E.dma_start(out=d_h[:, :], in_=h_sb[:, hp, :]), 16)
                    wait("vector", "dout")
                if n >= 1:
                    run("vector", "v", lambda E, n=n, hp=hp: E.tensor_copy(
                        out=state_sb[:, n - 1:n], in_=h_sb[:, 1 - hp, TBK - 1:TBK]))
                for half in range(2):
                    wait("tensor", "v", v_ym_use)
                    for qq in range(2):
                        run("tensor", "t", lambda E, par=par, n=n, half=half, qq=qq: E.matmul(
                            out=ps_cb[:, qq * 512:(qq + 1) * 512], lhsT=bcsel_sb[64:96, 16 + n, :],
                            rhs=xp_sb[64:96, xpar, half * 1024 + qq * 512:half * 1024 + (qq + 1) * 512],
                            start=True, stop=True))
                    t_cb = cnt["t"]
                    wait("vector", "t", t_cb)
                    if n == 0:
                        run("vector", "v", lambda E, par=par, half=half, hp=hp: E.tensor_tensor(
                            out=yacc_sb[:, par, half * 1024:(half + 1) * 1024],
                            in0=h_sb[:, hp, half * 1024:(half + 1) * 1024],
                            in1=ps_cb[:, :1024], op=OP.mult))
                    else:
                        run("vector", "v", lambda E, half=half, hp=hp: E.tensor_tensor(
                            out=bu_sb[:, half * 1024:(half + 1) * 1024],
                            in0=h_sb[:, hp, half * 1024:(half + 1) * 1024],
                            in1=ps_cb[:, :1024], op=OP.mult))
                        run("vector", "v", lambda E, par=par, half=half: E.tensor_tensor(
                            out=yacc_sb[:, par, half * 1024:(half + 1) * 1024],
                            in0=yacc_sb[:, par, half * 1024:(half + 1) * 1024],
                            in1=bu_sb[:, half * 1024:(half + 1) * 1024], op=OP.add))
                    v_ym_use = cnt["v"]
                if feeder is not None and n >= 6:
                    p = next(feeder, None)
                    if p is not None:
                        p()
            t_xp_use[xpar] = cnt["t"]
            run("vector", "v", lambda E: E.tensor_copy(
                out=state_sb[:, N - 1:N], in_=h_sb[:, (N - 1) % 2, TBK - 1:TBK]))
            vdrain()
            v_yacc = cnt["v"]
            if DEBUG and b == DBGBLK:
                wait("gpsimd", "v", v_yacc)
                run("gpsimd", "dout", lambda E, par=par: E.dma_start(out=d_yacc[:, :], in_=yacc_sb[:, par, :]), 16)
                wait("vector", "dout")
            wait("sync", "v", v_yacc)
            wait("sync", "cc", ncc_so_far)
            run("sync", "din", lambda E, b=b, par=par: E.dma_start(
                out=ybin[b, :, :], in_=yacc_sb[:, par, :]), 16)
            g_yacc[par] = cnt["din"]

        def emit_ag(b):
            nonlocal ncc_so_far
            wait("gpsimd", "g", cnt["g"])
            wait("gpsimd", "din", cnt["din"])
            run("gpsimd", "cc", lambda E, b=b: E.collective_compute(
                "AllGather", OP.bypass, replica_groups=[list(range(NCORE))],
                ins=[ybin[b].opt()], outs=[ybout[b].opt()]))
            ncc_so_far = cnt["cc"]



        for p in proj_pieces(0):
            p()
        for p in proj_pieces(1):
            p()
        for b in range(NBK):
            feeder = iter(proj_pieces(b + 2)) if b + 2 < NBK else None
            emit_scan(b, feeder)
            if feeder is not None:
                for p in feeder:
                    p()
            emit_ag(b)
        wait("gpsimd", "cc", NBK)
        wait("sync", "cc", NBK)
        def ycopy(E, SEM, piece):
            E.reg_add(REG["s"], REG["rb"], piece * (OWN // 4))
            E.dma_start(out=yown[:, piece * (OWN // 4):(piece + 1) * (OWN // 4)],
                        in_=bass.AP(ybout_h, REG["s"], [[TBK, D], [1, OWN // 4]])
                        ).then_inc(SEM["g"], 16)
        emit("gpsimd", lambda E, SEM: ycopy(E, SEM, 0))
        cnt["g"] += 16
        g_yown_p = [cnt["g"]]
        for piece in range(1, 4):
            emit("gpsimd", lambda E, SEM, piece=piece: ycopy(E, SEM, piece))
            cnt["g"] += 16
            g_yown_p.append(cnt["g"])

        dout_after_tc = [0, 0]
        v_z1_done = [v_setup, v_setup]
        t_tr_done = [0, 0]
        g_ytc_l = [0, 0]
        din_x_l = [0, 0]
        s_sq1 = [0, 0]
        s_sq2 = [0, 0]
        v_qlT_l = [0]

        def ffn_loads(tc):
            fp = tc % 2
            wait("gpsimd", "t", t_tr_done[fp])
            wait("gpsimd", "g", g_yown_p[min(tc // 4, 3)])
            run("gpsimd", "g", lambda E, tc=tc, fp=fp: E.dma_start(
                out=ytc_sb[:, fp, :].rearrange("p (s c) -> p s c", s=8),
                in_=yown.rearrange("(s p) t -> p s t", p=128)[:, :, tc * 128:(tc + 1) * 128]), 16)
            g_ytc_l[fp] = cnt["g"]
            wait("sync", "v", v_z1_done[fp])
            run("sync", "din", lambda E, tc=tc, fp=fp: E.dma_start(
                out=xstg_sb[:, fp, :], in_=xown[tc * 128:(tc + 1) * 128, :]), 16)
            din_x_l[fp] = cnt["din"]

        def stage1(tc):
            fp = tc % 2
            z1 = ab_sb[:, fp, 0:1024]
            wait("tensor", "g", g_ytc_l[fp])
            wait("tensor", "v", v_z1_done[1 - fp])
            for s8 in range(8):
                run("tensor", "t", lambda E, s8=s8, fp=fp: E.transpose(
                    out=ps_proj[:, s8 * 128:(s8 + 1) * 128],
                    in_=ytc_sb[:, fp, s8 * 128:(s8 + 1) * 128], identity=ident_sb[:, :]))
            t_ytr = cnt["t"]
            t_tr_done[fp] = t_ytr
            wait("vector", "t", t_ytr)
            wait("vector", "din", din_x_l[fp])
            run("vector", "v", lambda E, fp=fp, z1=z1: E.tensor_tensor(
                out=z1, in0=xstg_sb[:, fp, :], in1=ps_proj[:, :], op=OP.add))
            vdrain()
            v_z1_done[fp] = cnt["v"]
            vop(lambda E, fp=fp, z1=z1: E.bn_stats(out=bnst_sb[:, fp, 0:6], in_=z1[:, 0:512]))
            vop(lambda E, fp=fp, z1=z1: E.bn_stats(out=bnst_sb[:, fp, 6:12], in_=z1[:, 512:1024]))
            vop(lambda E, fp=fp, z1=z1: E.tensor_reduce(
                out=cols_sb[:, fp, 0:1], in_=z1, axis=AX.X, op=OP.max))
            vop(lambda E, fp=fp, z1=z1: E.tensor_reduce(
                out=cols_sb[:, fp, 1:2], in_=z1, axis=AX.X, op=OP.min))
            vdrain()
            vop(lambda E, fp=fp: E.bn_aggr(out=cols_sb[:, fp, 2:4], in_=bnst_sb[:, fp, :]))
            vdrain()
            v_var = cnt["v"]
            wait("scalar", "v", v_var)
            run("scalar", "s", lambda E, fp=fp: E.activation(
                out=cols_sb[:, fp, 5:6], in_=cols_sb[:, fp, 3:4], func=AF.Sqrt,
                bias=eps_sb[:, :], scale=1.0))
            s_sq1[fp] = cnt["s"]

        def stage2(tc):
            fp = tc % 2
            z1 = ab_sb[:, fp, 0:1024]
            z2 = ab_sb[:, fp, 1024:2048]
            ql = h_sb[:, fp, 0:1024]
            wait("vector", "s", s_sq1[fp])
            vop(lambda E, fp=fp: E.reciprocal(out=cols_sb[:, fp, 4:5], in_=cols_sb[:, fp, 5:6]))
            vdrain()
            vop(lambda E, fp=fp: E.tensor_scalar(
                out=cols_sb[:, fp, 16:17], in0=cols_sb[:, fp, 2:3],
                scalar1=cols_sb[:, fp, 4:5], scalar2=-1.0, op0=OP.mult, op1=OP.mult))
            vop(lambda E, fp=fp: E.tensor_scalar(
                out=cols_sb[:, fp, 6:7], in0=cols_sb[:, fp, 0:1],
                scalar1=cols_sb[:, fp, 2:3], scalar2=None, op0=OP.subtract))
            vop(lambda E, fp=fp: E.tensor_scalar(
                out=cols_sb[:, fp, 7:8], in0=cols_sb[:, fp, 2:3],
                scalar1=cols_sb[:, fp, 1:2], scalar2=None, op0=OP.subtract))
            vdrain()
            vop(lambda E, fp=fp: E.tensor_tensor(
                out=cols_sb[:, fp, 9:10], in0=cols_sb[:, fp, 6:7],
                in1=cols_sb[:, fp, 7:8], op=OP.max))
            vdrain()
            vop(lambda E, fp=fp: E.tensor_scalar(
                out=cols_sb[:, fp, 8:9], in0=cols_sb[:, fp, 9:10],
                scalar1=cols_sb[:, fp, 4:5], scalar2=1e-12, op0=OP.mult, op1=OP.max))
            vdrain()
            vop(lambda E, fp=fp: E.tensor_scalar(
                out=cols_sb[:, fp, 8:9], in0=cols_sb[:, fp, 8:9],
                scalar1=1.0 / 127.0, scalar2=None, op0=OP.mult))
            vdrain()
            vop(lambda E, fp=fp: E.reciprocal(out=cols_sb[:, fp, 10:11], in_=cols_sb[:, fp, 8:9]))
            vdrain()
            v_rr = cnt["v"]
            wait("scalar", "v", v_rr)
            run("scalar", "s", lambda E, fp=fp, z1=z1: E.activation(
                out=z1, in_=z1, func=AF.Identity,
                bias=cols_sb[:, fp, 16:17], scale=cols_sb[:, fp, 4:5]))
            run("scalar", "s", lambda E, fp=fp, z1=z1, ql=ql: E.activation(
                out=ql, in_=z1, func=AF.Identity,
                bias=rnd_sb[:, :], scale=cols_sb[:, fp, 10:11]))
            run("scalar", "s", lambda E, ql=ql: E.activation(
                out=ql, in_=ql, func=AF.Identity,
                bias=nrnd_sb[:, :], scale=1.0))
            s_ql = cnt["s"]
            wait("tensor", "s", s_ql)
            wait("tensor", "v", v_qlT_l[0])
            for g in range(8):
                run("tensor", "t", lambda E, g=g, ql=ql: E.transpose(
                    out=ps_bm[:, g * 128:(g + 1) * 128],
                    in_=ql[:, g * 128:(g + 1) * 128], identity=ident_sb[:, :]))
            t_trAll = cnt["t"]
            wait("vector", "t", t_trAll)
            run("vector", "v", lambda E: E.tensor_copy(
                out=qlT_sb[:, 0:8, :], in_=ps_bm[:, :]))
            vdrain()
            v_qlT = cnt["v"]
            v_qlT_l[0] = v_qlT
            wait("tensor", "v", v_qlT)
            for jb in range(2):
                for c in range(8):
                    run("tensor", "t", lambda E, c=c, jb=jb: E.matmul(
                        out=ps_cb[:, jb * 512:(jb + 1) * 512], lhsT=qlT_sb[:, c, :],
                        rhs=qrhs_sb[:, c, jb * 512:(jb + 1) * 512],
                        start=(c == 0), stop=(c == 7)))
                t_ffn = cnt["t"]
                wait("vector", "t", t_ffn)
                run("vector", "v", lambda E, jb=jb, fp=fp: E.scalar_tensor_tensor(
                    out=ffn_sb[:, jb * 512:(jb + 1) * 512], in0=ps_cb[:, jb * 512:(jb + 1) * 512],
                    scalar=cols_sb[:, fp, 8:9], in1=rs_bc[:, jb * 512:(jb + 1) * 512],
                    op0=OP.mult, op1=OP.mult))
            v_stt2 = cnt["v"]
            run("vector", "v", lambda E, z1=z1, z2=z2: E.tensor_tensor(
                out=z2, in0=z1, in1=ffn_sb[:, :], op=OP.add))
            vop(lambda E, fp=fp, z2=z2: E.bn_stats(out=bnst_sb[:, fp, 0:6], in_=z2[:, 0:512]))
            vop(lambda E, fp=fp, z2=z2: E.bn_stats(out=bnst_sb[:, fp, 6:12], in_=z2[:, 512:1024]))
            vdrain()
            vop(lambda E, fp=fp: E.bn_aggr(out=cols_sb[:, fp, 12:14], in_=bnst_sb[:, fp, :]))
            vdrain()
            v_var2 = cnt["v"]
            wait("scalar", "v", v_var2)
            run("scalar", "s", lambda E, fp=fp: E.activation(
                out=cols_sb[:, fp, 15:16], in_=cols_sb[:, fp, 13:14], func=AF.Sqrt,
                bias=eps_sb[:, :], scale=1.0))
            s_sq2[fp] = cnt["s"]

        def stage3(tc):
            fp = tc % 2
            z2 = ab_sb[:, fp, 1024:2048]
            wait("vector", "s", s_sq2[fp])
            vop(lambda E, fp=fp: E.reciprocal(out=cols_sb[:, fp, 14:15], in_=cols_sb[:, fp, 15:16]))
            vdrain()
            vop(lambda E, fp=fp: E.tensor_scalar(
                out=cols_sb[:, fp, 17:18], in0=cols_sb[:, fp, 12:13],
                scalar1=cols_sb[:, fp, 14:15], scalar2=-1.0, op0=OP.mult, op1=OP.mult))
            vdrain()
            v_nb2 = cnt["v"]
            wait("scalar", "v", v_nb2)
            wait("scalar", "dout", dout_after_tc[fp])
            run("scalar", "s", lambda E, fp=fp, z2=z2: E.activation(
                out=u_sb[:, fp, 0:D], in_=z2, func=AF.Identity,
                bias=cols_sb[:, fp, 17:18], scale=cols_sb[:, fp, 14:15]))
            s_out = cnt["s"]
            wait("gpsimd", "s", s_out)
            run("gpsimd", "dout", lambda E, tc=tc, fp=fp: E.dma_start(
                out=out[tc * 128:(tc + 1) * 128, :], in_=u_sb[:, fp, 0:D]), 16)
            dout_after_tc[fp] = cnt["dout"]

        ffn_loads(0)
        ffn_loads(1)
        stage1(0)
        for tc in range(16):
            if tc + 2 < 16:
                ffn_loads(tc + 2)
            if tc + 1 < 16:
                stage1(tc + 1)
            stage2(tc)
            if tc >= 1:
                stage3(tc - 1)
        stage3(15)
        wait("gpsimd", "dout", cnt["dout"])

        @blk_.sync
        def _(E):
            for f in ops["sync"]:
                f(E, SEMH)

        @blk_.tensor
        def _(E):
            for f in ops["tensor"]:
                f(E, SEMH)

        @blk_.scalar
        def _(E):
            for f in ops["scalar"]:
                f(E, SEMH)

        @blk_.vector
        def _(E):
            for f in ops["vector"]:
                f(E, SEMH)

        @blk_.gpsimd
        def _(E):
            for f in ops["gpsimd"]:
                f(E, SEMH)

    return nc


def kernel(L0_attn_x, L0_attn_W_xproj, L0_attn_W_dt, L0_attn_b_dt, L0_attn_A_log,
           L0_ffn_rhs):
    x = np.asarray(L0_attn_x, dtype=np.float32).reshape(T, D)
    wxp0 = np.asarray(L0_attn_W_xproj, dtype=np.float32)
    wxp = np.ascontiguousarray(
        np.concatenate([wxp0[:, 32:96], wxp0[:, 0:16], wxp0[:, 16:32]], axis=1))
    wdt = np.asarray(L0_attn_W_dt, dtype=np.float32)
    bdt = np.asarray(L0_attn_b_dt, dtype=np.float32)
    alog = np.asarray(L0_attn_A_log, dtype=np.float32)
    rhs = np.ascontiguousarray(np.asarray(L0_ffn_rhs, dtype=np.float32))
    rhsT = np.ascontiguousarray(rhs.T)
    xtf = np.ascontiguousarray(x.T)
    ident = np.eye(128, dtype=np.float32)
    bcsel = np.zeros((32, 32, 128), dtype=np.float32)
    bcsel[np.arange(32), np.arange(32), :] = 1.0

    nc = build_program()
    in_maps = []
    for k in range(NCORE):
        dsl = slice(k * 128, (k + 1) * 128)
        in_maps.append({
            "xtf": xtf,
            "xtd": np.ascontiguousarray(xtf[dsl, :]),
            "xown": np.ascontiguousarray(x[k * OWN:(k + 1) * OWN, :]),
            "wxp": wxp,
            "wdtd": np.ascontiguousarray(wdt[:, dsl]),
            "bdtd": np.ascontiguousarray(bdt[dsl].reshape(128, 1)),
            "alogd": np.ascontiguousarray(alog.T[dsl, :]),
            "rhs": rhs, "rhsT": rhsT, "ident": ident, "bcsel": bcsel,
        })
    res = run_bass_kernel_spmd(nc, in_maps, core_ids=list(range(NCORE)))
    shards = [res.results[c]["out"] for c in range(NCORE)]
    return np.concatenate(shards, axis=0).reshape(B, S, D).astype(np.float32)


# version 28
# speedup vs baseline: 1.2133x; 1.0445x over previous
import contextlib

import numpy as np
import concourse.bass as bass
import concourse.mybir as mybir
from concourse.bass_types import DynSlice
from concourse.bass_utils import run_bass_kernel_spmd

F32 = mybir.dt.float32
BF16 = mybir.dt.bfloat16
AF = mybir.ActivationFunctionType
OP = mybir.AluOpType
AX = mybir.AxisListType

B, S, D, N, RK = 2, 8192, 1024, 16, 64
NCORE = 8
T = B * S
OWN = T // NCORE
TBK = 2048
NBK = T // TBK
QT = 4096
RND = 12582912.0
EPS = 1e-5
ENGS = ("sync", "tensor", "scalar", "vector", "gpsimd")
DEBUG = False
DBGBLK = 0


def build_program():
    nc = bass.Bass("TRN2", target_bir_lowering=False)

    xtf = nc.dram_tensor("xtf", [D, T], F32, kind="ExternalInput").ap()
    xtd = nc.dram_tensor("xtd", [128, T], F32, kind="ExternalInput").ap()
    xown = nc.dram_tensor("xown", [OWN, D], F32, kind="ExternalInput").ap()
    wxp = nc.dram_tensor("wxp", [D, 96], F32, kind="ExternalInput").ap()
    wdtd = nc.dram_tensor("wdtd", [RK, 128], F32, kind="ExternalInput").ap()
    bdtd = nc.dram_tensor("bdtd", [128, 1], F32, kind="ExternalInput").ap()
    alogd = nc.dram_tensor("alogd", [128, N], F32, kind="ExternalInput").ap()
    rhs = nc.dram_tensor("rhs", [D, D], F32, kind="ExternalInput").ap()
    rhsT = nc.dram_tensor("rhsT", [D, D], F32, kind="ExternalInput").ap()
    ident = nc.dram_tensor("ident", [128, 128], F32, kind="ExternalInput").ap()
    bcsel = nc.dram_tensor("bcsel", [32, 32, 128], F32, kind="ExternalInput").ap()
    out = nc.dram_tensor("out", [OWN, D], F32, kind="ExternalOutput").ap()
    if DEBUG:
        d_xp = nc.dram_tensor("d_xp", [96, TBK], F32, kind="ExternalOutput").ap()
        d_dt = nc.dram_tensor("d_dt", [128, TBK], F32, kind="ExternalOutput").ap()
        d_u = nc.dram_tensor("d_u", [128, TBK], F32, kind="ExternalOutput").ap()
        d_ab = nc.dram_tensor("d_ab", [128, TBK], F32, kind="ExternalOutput").ap()
        d_bu = nc.dram_tensor("d_bu", [128, TBK], F32, kind="ExternalOutput").ap()
        d_h = nc.dram_tensor("d_h", [128, TBK], F32, kind="ExternalOutput").ap()
        d_yacc = nc.dram_tensor("d_yacc", [128, TBK], F32, kind="ExternalOutput").ap()
        d_ytc = nc.dram_tensor("d_ytc", [128, D], F32, kind="ExternalOutput").ap()
        d_z1 = nc.dram_tensor("d_z1", [128, D], F32, kind="ExternalOutput").ap()
        d_z1n = nc.dram_tensor("d_z1n", [128, D], F32, kind="ExternalOutput").ap()
        d_ql = nc.dram_tensor("d_ql", [128, D], F32, kind="ExternalOutput").ap()
        d_ffn = nc.dram_tensor("d_ffn", [128, D], F32, kind="ExternalOutput").ap()
        d_cols = nc.dram_tensor("d_cols", [128, 16], F32, kind="ExternalOutput").ap()
        d_qrhs = nc.dram_tensor("d_qrhs", [128, 8, D], F32, kind="ExternalOutput").ap()
        d_rsbc = nc.dram_tensor("d_rsbc", [128, D], F32, kind="ExternalOutput").ap()
    ybin_h = nc.dram_tensor("ybin", [NBK, 128, TBK], F32)
    ybout_h = nc.dram_tensor("ybout", [NBK, D, TBK], F32)
    yown_h = nc.dram_tensor("yown", [D, OWN], F32)
    ybin = ybin_h.ap()
    ybout = ybout_h.ap()
    yown = yown_h.ap()

    ops = {e: [] for e in ENGS}
    cnt = {c: 0 for c in ("din", "g", "cc", "t", "s", "v", "dout")}

    def emit(e, fn):
        ops[e].append(fn)

    def wait(e, c, val=None):
        v = cnt[c] if val is None else val
        if v > 0:
            emit(e, lambda E, SEM, c=c, v=v: E.wait_ge(SEM[c], v))

    def run(e, c, fn, n=1):
        emit(e, lambda E, SEM, fn=fn, c=c, n=n: fn(E).then_inc(SEM[c], n))
        cnt[c] += n

    def vop(fn):
        emit("vector", lambda E, SEM, fn=fn: fn(E))

    def vdrain():
        emit("vector", lambda E, SEM: E.drain(fusable=False))
        emit("vector", lambda E, SEM: E.sem_inc(SEM["v"], 1))
        cnt["v"] += 1

    with contextlib.ExitStack() as ctx:
        ent = ctx.enter_context
        wxp_sb = ent(nc.sbuf_tensor("wxp_sb", [128, 8, 96], BF16))
        wdt_sb = ent(nc.sbuf_tensor("wdt_sb", [RK, 128], BF16))
        bdt_sb = ent(nc.sbuf_tensor("bdt_sb", [128, 1], F32))
        alog_sb = ent(nc.sbuf_tensor("alog_sb", [128, N], F32))
        a_sb = ent(nc.sbuf_tensor("a_sb", [128, N], F32))
        ident_sb = ent(nc.sbuf_tensor("ident_sb", [128, 128], F32))
        bcsel_sb = ent(nc.sbuf_tensor("bcsel_sb", [96, 32, 128], BF16))
        bcself_sb = ent(nc.sbuf_tensor("bcself_sb", [72, 8, 128], F32))
        rs_col = ent(nc.sbuf_tensor("rs_col", [128, 8], F32))
        rr_col = ent(nc.sbuf_tensor("rr_col", [128, 8], F32))
        rsrow_tmp = ent(nc.sbuf_tensor("rsrow_tmp", [8, 2, 128], F32))
        rsrow_sb = ent(nc.sbuf_tensor("rsrow_sb", [72, 2, 128], F32))
        rs_bc = ent(nc.sbuf_tensor("rs_bc", [128, D], F32))
        rr_bc = ent(nc.sbuf_tensor("rr_bc", [128, D], F32))
        qrhs_sb = ent(nc.sbuf_tensor("qrhs_sb", [128, 8, D], BF16))
        xtb_sb = ent(nc.sbuf_tensor("xtb_sb", [128, 2, TBK], BF16))
        xtd_sb = ent(nc.sbuf_tensor("xtd_sb", [128, 1, TBK], F32))
        xp_sb = ent(nc.sbuf_tensor("xp_sb", [96, 3, TBK], BF16))
        dt_sb = ent(nc.sbuf_tensor("dt_sb", [128, 3, TBK], F32))
        u_sb = ent(nc.sbuf_tensor("u_sb", [128, 3, TBK], F32))
        ab_sb = ent(nc.sbuf_tensor("ab_sb", [128, 2, TBK], F32))
        bu_sb = ent(nc.sbuf_tensor("bu_sb", [128, TBK], F32))
        h_sb = ent(nc.sbuf_tensor("h_sb", [128, 2, TBK], F32))
        yacc_sb = ent(nc.sbuf_tensor("yacc_sb", [128, 2, TBK], F32))
        state_sb = ent(nc.sbuf_tensor("state_sb", [128, N], F32))
        ytc_sb = ent(nc.sbuf_tensor("ytc_sb", [128, 2, D], F32))
        xstg_sb = ent(nc.sbuf_tensor("xstg_sb", [128, 2, D], F32))
        z1_sb = ent(nc.sbuf_tensor("z1_sb", [128, D], F32))
        z2_sb = ent(nc.sbuf_tensor("z2_sb", [128, D], F32))
        ql_sb = ent(nc.sbuf_tensor("ql_sb", [128, D], F32))
        qlT_sb = ent(nc.sbuf_tensor("qlT_sb", [128, 8, 128], BF16))
        ffn_sb = ent(nc.sbuf_tensor("ffn_sb", [128, D], F32))
        cols_sb = ent(nc.sbuf_tensor("cols_sb", [128, 2, 20], F32))
        bnst_sb = ent(nc.sbuf_tensor("bnst_sb", [128, 2, 12], F32))
        eps_sb = ent(nc.sbuf_tensor("eps_sb", [128, 1], F32))
        one_sb = ent(nc.sbuf_tensor("one_sb", [128, 1], F32))
        rnd_sb = ent(nc.sbuf_tensor("rnd_sb", [128, 1], F32))
        nrnd_sb = ent(nc.sbuf_tensor("nrnd_sb", [128, 1], F32))
        ps_proj = ent(nc.psum_tensor("ps_proj", [128, 1024], F32))
        ps_bm = ent(nc.psum_tensor("ps_bm", [128, 1024], F32))
        ps_cb = ent(nc.psum_tensor("ps_cb", [128, 1024], F32))
        ps_tr = ent(nc.psum_tensor("ps_tr", [128, 512], F32))
        ps_f = ent(nc.psum_tensor("ps_f", [128, 512], F32))
        sm_din = ent(nc.semaphore("sm_din"))
        sm_g = ent(nc.semaphore("sm_g"))
        sm_cc = ent(nc.semaphore("sm_cc"))
        sm_t = ent(nc.semaphore("sm_t"))
        sm_s = ent(nc.semaphore("sm_s"))
        sm_v = ent(nc.semaphore("sm_v"))
        sm_dout = ent(nc.semaphore("sm_dout"))
        blk_ = ent(nc.Block())
        SEMH = {"din": sm_din, "g": sm_g, "cc": sm_cc, "t": sm_t, "s": sm_s,
                "v": sm_v, "dout": sm_dout}

        REG = {}

        def initreg(E, SEM):
            pid = E.partition_id()
            rbv = E.compute_val(pid * (D * TBK))
            REG["rb"] = rbv
            REG["s"] = E.alloc_register("yoffreg")
        emit("gpsimd", initreg)
        run("gpsimd", "g", lambda E: E.dma_start(
            out=wxp_sb[:, :, :], in_=wxp.rearrange("(c p) f -> p c f", p=128)), 16)
        run("gpsimd", "g", lambda E: E.dma_start(out=wdt_sb[:, :], in_=wdtd[:, :]), 16)
        run("gpsimd", "g", lambda E: E.dma_start(
            out=bcsel_sb[64:96, :, :], in_=bcsel[:, :, :]), 16)
        run("sync", "din", lambda E: E.dma_start(out=bdt_sb[:, :], in_=bdtd[:, :]), 16)
        run("sync", "din", lambda E: E.dma_start(out=alog_sb[:, :], in_=alogd[:, :]), 16)
        run("sync", "din", lambda E: E.dma_start(out=ident_sb[:, :], in_=ident[:, :]), 16)
        run("sync", "din", lambda E: E.dma_start(
            out=bcself_sb[64:72, :, :], in_=bcsel[0:8, 0:8, :]), 16)
        setup_din = cnt["din"]
        setup_g = cnt["g"]

        wait("scalar", "din", setup_din)
        run("scalar", "s", lambda E: E.activation(
            out=a_sb[:, :], in_=alog_sb[:, :], func=AF.Exp))
        wait("vector", "s")
        run("vector", "v", lambda E: E.tensor_scalar(
            out=a_sb[:, :], in0=a_sb[:, :], scalar1=-1.0, scalar2=None, op0=OP.mult))
        vdrain()
        v_neg = cnt["v"]

        for g in range(8):
            wait("sync", "v")
            run("sync", "din", lambda E, g=g: E.dma_start(
                out=ffn_sb[:, :], in_=rhsT[g * 128:(g + 1) * 128, :]), 16)
            wait("vector", "din")
            vop(lambda E, g=g: E.tensor_reduce(
                out=rs_col[:, g:g + 1], in_=ffn_sb[:, :], axis=AX.X, op=OP.max,
                apply_absolute_value=True))
            vdrain()
        vop(lambda E: E.tensor_scalar(
            out=rs_col[:, :], in0=rs_col[:, :], scalar1=1e-12, scalar2=1.0 / 127.0,
            op0=OP.max, op1=OP.mult))
        vdrain()
        vop(lambda E: E.reciprocal(out=rr_col[:, :], in_=rs_col[:, :]))
        vdrain()
        v_rs_done = cnt["v"]
        wait("tensor", "din", setup_din)
        wait("tensor", "v", v_rs_done)
        run("tensor", "t", lambda E: E.transpose(
            out=ps_tr[:8, 0:128], in_=rs_col[:, :], identity=ident_sb[:, :]))
        run("tensor", "t", lambda E: E.transpose(
            out=ps_tr[:8, 128:256], in_=rr_col[:, :], identity=ident_sb[:, :]))
        t_rows = cnt["t"]
        wait("vector", "t", t_rows)
        vop(lambda E: E.tensor_copy(out=rsrow_tmp[:, 0, :], in_=ps_tr[:8, 0:128]))
        vop(lambda E: E.tensor_copy(out=rsrow_tmp[:, 1, :], in_=ps_tr[:8, 128:256]))
        vdrain()
        v_rows = cnt["v"]
        wait("sync", "v", v_rows)
        run("sync", "din", lambda E: E.dma_start(
            out=rsrow_sb[64:72, :, :], in_=rsrow_tmp[:, :, :]), 16)
        rows_din = cnt["din"]
        wait("tensor", "din", rows_din)
        for g in range(8):
            run("tensor", "t", lambda E, g=g: E.matmul(
                out=ps_bm[:, g * 128:(g + 1) * 128] if g < 8 else None,
                lhsT=bcself_sb[64:72, g, :],
                rhs=rsrow_sb[64:72, 0, :], start=True, stop=True))
        for g in range(8):
            run("tensor", "t", lambda E, g=g: E.matmul(
                out=ps_cb[:, g * 128:(g + 1) * 128], lhsT=bcself_sb[64:72, g, :],
                rhs=rsrow_sb[64:72, 1, :], start=True, stop=True))
        t_bc = cnt["t"]
        wait("vector", "t", t_bc)
        vop(lambda E: E.tensor_copy(out=rs_bc[:, 0:1024], in_=ps_bm[:, 0:1024]))
        vop(lambda E: E.tensor_copy(out=rr_bc[:, 0:1024], in_=ps_cb[:, 0:1024]))
        vdrain()
        v_rsbc = cnt["v"]
        for g in range(8):
            wait("sync", "v")
            run("sync", "din", lambda E, g=g: E.dma_start(
                out=ffn_sb[:, :], in_=rhs[g * 128:(g + 1) * 128, :]), 16)
            wait("vector", "din")
            vop(lambda E, g=g: E.tensor_tensor(
                out=ffn_sb[:, :], in0=ffn_sb[:, :], in1=rr_bc[:, :], op=OP.mult))
            vop(lambda E, g=g: E.tensor_scalar(
                out=ffn_sb[:, :], in0=ffn_sb[:, :], scalar1=RND, scalar2=None,
                op0=OP.add))
            vop(lambda E, g=g: E.tensor_scalar(
                out=qrhs_sb[:, g, :], in0=ffn_sb[:, :], scalar1=-RND, scalar2=None,
                op0=OP.add))
            vdrain()
        run("vector", "v", lambda E: E.memset(state_sb[:, :], 0.0))
        run("vector", "v", lambda E: E.memset(eps_sb[:, :], EPS))
        run("vector", "v", lambda E: E.memset(one_sb[:, :], 1.0))
        run("vector", "v", lambda E: E.memset(rnd_sb[:, :], RND))
        run("vector", "v", lambda E: E.memset(nrnd_sb[:, :], -RND))
        v_setup = cnt["v"]


        v_after_scan = [v_neg, v_neg]
        t_xtb_use = [0, 0]
        v_bu_use = v_rsbc
        v_ym_use = v_rsbc
        g_yacc = [0, 0]
        s_dt = [0, 0, 0]
        v_u = [0, 0, 0]
        v_xp_l = [0, 0, 0]
        v_xp_last = 0
        t_xp_use = [0, 0, 0]

        ncc_so_far = 0
        def proj_pieces(b):
            par = b % 2
            xpar = b % 3
            dpar = b % 3
            st = {}
            pieces = []
            def mk_c(c):
                def piece():
                    nonlocal v_xp_last
                    sl = c % 2
                    if c == 0:
                        wait("gpsimd", "cc", ncc_so_far)
                        wait("tensor", "v", max(v_setup, v_xp_last))
                    wait("gpsimd", "t", t_xtb_use[sl])
                    run("gpsimd", "g", lambda E, c=c, sl=sl, b=b: E.dma_start(
                        out=xtb_sb[:, sl, :],
                        in_=xtf[c * 128:(c + 1) * 128, b * TBK:(b + 1) * TBK]), 16)
                    wait("tensor", "g")
                    for qq in range(2):
                        run("tensor", "t", lambda E, c=c, sl=sl, qq=qq: E.matmul(
                            out=ps_proj[:96, qq * 512:(qq + 1) * 512], lhsT=wxp_sb[:, c, :],
                            rhs=xtb_sb[:, sl, qq * 512:(qq + 1) * 512],
                            start=(c == 0), stop=(c == 7)))
                    for qq in range(2):
                        psx = ps_tr if qq == 0 else ps_f
                        run("tensor", "t", lambda E, c=c, sl=sl, qq=qq, psx=psx: E.matmul(
                            out=psx[:96, 0:512], lhsT=wxp_sb[:, c, :],
                            rhs=xtb_sb[:, sl, 1024 + qq * 512:1024 + (qq + 1) * 512],
                            start=(c == 0), stop=(c == 7)))
                    t_xtb_use[sl] = cnt["t"]
                    if c == 7:
                        st["t_xpmm"] = cnt["t"]
                return piece
            for c in range(8):
                pieces.append(mk_c(c))
            def piece_evac():
                nonlocal v_xp_last
                wait("vector", "t", max(st["t_xpmm"], t_xp_use[xpar]))
                vop(lambda E, xpar=xpar: E.tensor_copy(
                    out=xp_sb[:, xpar, 0:1024], in_=ps_proj[:96, :1024]))
                vop(lambda E, xpar=xpar: E.tensor_copy(
                    out=xp_sb[:, xpar, 1024:1536], in_=ps_tr[:96, 0:512]))
                vop(lambda E, xpar=xpar: E.tensor_copy(
                    out=xp_sb[:, xpar, 1536:2048], in_=ps_f[:96, 0:512]))
                vdrain()
                st["v_xp"] = cnt["v"]
                v_xp_last = cnt["v"]
            pieces.append(piece_evac)
            def mk_dt(half):
                def piece():
                    wait("tensor", "v", st["v_xp"])
                    wait("tensor", "g", setup_g)
                    for qq in range(2):
                        psx = ps_tr if qq == 0 else ps_f
                        run("tensor", "t", lambda E, xpar=xpar, half=half, qq=qq, psx=psx: E.matmul(
                            out=psx[:, 0:512], lhsT=wdt_sb[:, :],
                            rhs=xp_sb[0:64, xpar, half * 1024 + qq * 512:half * 1024 + (qq + 1) * 512],
                            start=True, stop=True))
                    t_dt = cnt["t"]
                    wait("scalar", "t", t_dt)
                    wait("scalar", "v", v_u[dpar])
                    for qq in range(2):
                        psx = ps_tr if qq == 0 else ps_f
                        c0 = half * 1024 + qq * 512
                        run("scalar", "s", lambda E, dpar=dpar, c0=c0, psx=psx: E.activation(
                            out=dt_sb[:, dpar, c0:c0 + 512], in_=psx[:, 0:512],
                            func=AF.Exp, bias=bdt_sb[:, :], scale=1.0))
                        run("scalar", "s", lambda E, dpar=dpar, c0=c0: E.activation(
                            out=dt_sb[:, dpar, c0:c0 + 512],
                            in_=dt_sb[:, dpar, c0:c0 + 512],
                            func=AF.Ln, bias=one_sb[:, :], scale=1.0))
                    wait("tensor", "s")
                    if half == 1:
                        s_dt[dpar] = cnt["s"]
                return piece
            pieces.append(mk_dt(0))
            pieces.append(mk_dt(1))
            def piece_u():
                wait("sync", "v", max(v_u))
                wait("sync", "cc", ncc_so_far)
                run("sync", "din", lambda E, b=b: E.dma_start(
                    out=xtd_sb[:, 0, :], in_=xtd[:, b * TBK:(b + 1) * TBK]), 16)
                din_xtd = cnt["din"]
                wait("vector", "s", s_dt[dpar])
                wait("vector", "din", din_xtd)
                run("vector", "v", lambda E, dpar=dpar: E.tensor_tensor(
                    out=u_sb[:, dpar, :], in0=dt_sb[:, dpar, :], in1=xtd_sb[:, 0, :],
                    op=OP.mult))
                v_u[dpar] = cnt["v"]
            pieces.append(piece_u)
            return pieces

        def emit_scan(b, feeder=None):
            nonlocal ncc_so_far, v_bu_use, v_ym_use
            par = b % 2
            xpar = b % 3
            dpar = b % 3
            q = b // 2
            wait("vector", "din", g_yacc[par])
            if b == 4:
                run("vector", "v", lambda E: E.memset(state_sb[:, :], 0.0))

            for n in range(N):
                ct = b * N + n
                abp = ct % 2
                hp = n % 2
                wait("scalar", "v", v_after_scan[abp])
                run("scalar", "s", lambda E, dpar=dpar, n=n, abp=abp: E.activation(
                    out=ab_sb[:, abp, :], in_=dt_sb[:, dpar, :], func=AF.Exp,
                    scale=a_sb[:, n:n + 1]))
                s_ab = cnt["s"]
                for half in range(2):
                    wait("tensor", "v", v_bu_use)
                    for qq in range(2):
                        run("tensor", "t", lambda E, par=par, n=n, half=half, qq=qq: E.matmul(
                            out=ps_bm[:, qq * 512:(qq + 1) * 512], lhsT=bcsel_sb[64:96, n, :],
                            rhs=xp_sb[64:96, xpar, half * 1024 + qq * 512:half * 1024 + (qq + 1) * 512],
                            start=True, stop=True))
                    t_bm = cnt["t"]
                    wait("vector", "t", t_bm)
                    run("vector", "v", lambda E, dpar=dpar, half=half: E.tensor_tensor(
                        out=bu_sb[:, half * 1024:(half + 1) * 1024],
                        in0=u_sb[:, dpar, half * 1024:(half + 1) * 1024],
                        in1=ps_bm[:, :1024], op=OP.mult))
                    v_bu_use = cnt["v"]
                wait("vector", "s", s_ab)
                if DEBUG and b == DBGBLK and n == 0:
                    wait("gpsimd", "v", cnt["v"])
                    wait("gpsimd", "s", cnt["s"])
                    run("gpsimd", "dout", lambda E, abp=abp: E.dma_start(out=d_ab[:, :], in_=ab_sb[:, abp, :]), 16)
                    run("gpsimd", "dout", lambda E: E.dma_start(out=d_bu[:, :], in_=bu_sb[:, :]), 16)
                    wait("vector", "dout")
                run("vector", "v", lambda E, n=n, abp=abp, hp=hp: E.tensor_tensor_scan(
                    out=h_sb[:, hp, :], data0=ab_sb[:, abp, :], data1=bu_sb[:, :],
                    initial=state_sb[:, n:n + 1], op0=OP.mult, op1=OP.add))
                v_after_scan[abp] = cnt["v"]
                if DEBUG and b == DBGBLK and n == 0:
                    vdrain()
                    wait("gpsimd", "v", cnt["v"])
                    run("gpsimd", "dout", lambda E, hp=hp: E.dma_start(out=d_h[:, :], in_=h_sb[:, hp, :]), 16)
                    wait("vector", "dout")
                if n >= 1:
                    run("vector", "v", lambda E, n=n, hp=hp: E.tensor_copy(
                        out=state_sb[:, n - 1:n], in_=h_sb[:, 1 - hp, TBK - 1:TBK]))
                for half in range(2):
                    wait("tensor", "v", v_ym_use)
                    for qq in range(2):
                        run("tensor", "t", lambda E, par=par, n=n, half=half, qq=qq: E.matmul(
                            out=ps_cb[:, qq * 512:(qq + 1) * 512], lhsT=bcsel_sb[64:96, 16 + n, :],
                            rhs=xp_sb[64:96, xpar, half * 1024 + qq * 512:half * 1024 + (qq + 1) * 512],
                            start=True, stop=True))
                    t_cb = cnt["t"]
                    wait("vector", "t", t_cb)
                    if n == 0:
                        run("vector", "v", lambda E, par=par, half=half, hp=hp: E.tensor_tensor(
                            out=yacc_sb[:, par, half * 1024:(half + 1) * 1024],
                            in0=h_sb[:, hp, half * 1024:(half + 1) * 1024],
                            in1=ps_cb[:, :1024], op=OP.mult))
                    else:
                        run("vector", "v", lambda E, half=half, hp=hp: E.tensor_tensor(
                            out=bu_sb[:, half * 1024:(half + 1) * 1024],
                            in0=h_sb[:, hp, half * 1024:(half + 1) * 1024],
                            in1=ps_cb[:, :1024], op=OP.mult))
                        run("vector", "v", lambda E, par=par, half=half: E.tensor_tensor(
                            out=yacc_sb[:, par, half * 1024:(half + 1) * 1024],
                            in0=yacc_sb[:, par, half * 1024:(half + 1) * 1024],
                            in1=bu_sb[:, half * 1024:(half + 1) * 1024], op=OP.add))
                    v_ym_use = cnt["v"]
                if feeder is not None and n >= 6:
                    p = next(feeder, None)
                    if p is not None:
                        p()
            t_xp_use[xpar] = cnt["t"]
            run("vector", "v", lambda E: E.tensor_copy(
                out=state_sb[:, N - 1:N], in_=h_sb[:, (N - 1) % 2, TBK - 1:TBK]))
            vdrain()
            v_yacc = cnt["v"]
            if DEBUG and b == DBGBLK:
                wait("gpsimd", "v", v_yacc)
                run("gpsimd", "dout", lambda E, par=par: E.dma_start(out=d_yacc[:, :], in_=yacc_sb[:, par, :]), 16)
                wait("vector", "dout")
            wait("sync", "v", v_yacc)
            wait("sync", "cc", ncc_so_far)
            run("sync", "din", lambda E, b=b, par=par: E.dma_start(
                out=ybin[b, :, :], in_=yacc_sb[:, par, :]), 16)
            g_yacc[par] = cnt["din"]

        def emit_ag(b):
            nonlocal ncc_so_far
            wait("gpsimd", "g", cnt["g"])
            wait("gpsimd", "din", cnt["din"])
            run("gpsimd", "cc", lambda E, b=b: E.collective_compute(
                "AllGather", OP.bypass, replica_groups=[list(range(NCORE))],
                ins=[ybin[b].opt()], outs=[ybout[b].opt()]))
            ncc_so_far = cnt["cc"]



        for p in proj_pieces(0):
            p()
        for p in proj_pieces(1):
            p()
        for b in range(NBK):
            feeder = iter(proj_pieces(b + 2)) if b + 2 < NBK else None
            emit_scan(b, feeder)
            if feeder is not None:
                for p in feeder:
                    p()
            emit_ag(b)
        wait("gpsimd", "cc", NBK)
        wait("sync", "cc", NBK)
        def ycopy(E, SEM, piece):
            E.reg_add(REG["s"], REG["rb"], piece * (OWN // 4))
            E.dma_start(out=yown[:, piece * (OWN // 4):(piece + 1) * (OWN // 4)],
                        in_=bass.AP(ybout_h, REG["s"], [[TBK, D], [1, OWN // 4]])
                        ).then_inc(SEM["g"], 16)
        emit("gpsimd", lambda E, SEM: ycopy(E, SEM, 0))
        cnt["g"] += 16
        g_yown_p = [cnt["g"]]
        for piece in range(1, 4):
            emit("gpsimd", lambda E, SEM, piece=piece: ycopy(E, SEM, piece))
            cnt["g"] += 16
            g_yown_p.append(cnt["g"])

        dout_after_tc = [0, 0]
        v_z1_done = [v_setup, v_setup]
        t_tr_done = [0, 0]
        g_ytc_l = [0, 0]
        din_x_l = [0, 0]
        s_sq1 = [0, 0]
        s_sq2 = [0, 0]
        v_qlT_l = [0]
        s_ql_l = [0]

        def ffn_loads(tc):
            fp = tc % 2
            wait("gpsimd", "t", t_tr_done[fp])
            wait("gpsimd", "g", g_yown_p[min(tc // 4, 3)])
            run("gpsimd", "g", lambda E, tc=tc, fp=fp: E.dma_start(
                out=ytc_sb[:, fp, :].rearrange("p (s c) -> p s c", s=8),
                in_=yown.rearrange("(s p) t -> p s t", p=128)[:, :, tc * 128:(tc + 1) * 128]), 16)
            g_ytc_l[fp] = cnt["g"]
            wait("sync", "v", v_z1_done[fp])
            run("sync", "din", lambda E, tc=tc, fp=fp: E.dma_start(
                out=xstg_sb[:, fp, :], in_=xown[tc * 128:(tc + 1) * 128, :]), 16)
            din_x_l[fp] = cnt["din"]

        def stage1(tc):
            fp = tc % 2
            z1 = ab_sb[:, fp, 0:1024]
            wait("tensor", "g", g_ytc_l[fp])
            wait("tensor", "v", v_z1_done[1 - fp])
            for s8 in range(8):
                run("tensor", "t", lambda E, s8=s8, fp=fp: E.transpose(
                    out=ps_proj[:, s8 * 128:(s8 + 1) * 128],
                    in_=ytc_sb[:, fp, s8 * 128:(s8 + 1) * 128], identity=ident_sb[:, :]))
            t_ytr = cnt["t"]
            t_tr_done[fp] = t_ytr
            wait("vector", "t", t_ytr)
            wait("vector", "din", din_x_l[fp])
            run("vector", "v", lambda E, fp=fp, z1=z1: E.tensor_tensor(
                out=z1, in0=xstg_sb[:, fp, :], in1=ps_proj[:, :], op=OP.add))
            vdrain()
            v_z1_done[fp] = cnt["v"]
            vop(lambda E, fp=fp, z1=z1: E.bn_stats(out=bnst_sb[:, fp, 0:6], in_=z1[:, 0:512]))
            vop(lambda E, fp=fp, z1=z1: E.bn_stats(out=bnst_sb[:, fp, 6:12], in_=z1[:, 512:1024]))
            vop(lambda E, fp=fp, z1=z1: E.tensor_reduce(
                out=cols_sb[:, fp, 0:1], in_=z1, axis=AX.X, op=OP.max))
            vop(lambda E, fp=fp, z1=z1: E.tensor_reduce(
                out=cols_sb[:, fp, 1:2], in_=z1, axis=AX.X, op=OP.min))
            vdrain()
            vop(lambda E, fp=fp: E.bn_aggr(out=cols_sb[:, fp, 2:4], in_=bnst_sb[:, fp, :]))
            vdrain()
            v_var = cnt["v"]
            wait("scalar", "v", v_var)
            run("scalar", "s", lambda E, fp=fp: E.activation(
                out=cols_sb[:, fp, 5:6], in_=cols_sb[:, fp, 3:4], func=AF.Sqrt,
                bias=eps_sb[:, :], scale=1.0))
            s_sq1[fp] = cnt["s"]

        def stage2a(tc):
            fp = tc % 2
            z1 = ab_sb[:, fp, 0:1024]
            z2 = ab_sb[:, fp, 1024:2048]
            ql = h_sb[:, fp, 0:1024]
            wait("vector", "s", s_sq1[fp])
            vop(lambda E, fp=fp: E.reciprocal(out=cols_sb[:, fp, 4:5], in_=cols_sb[:, fp, 5:6]))
            vdrain()
            vop(lambda E, fp=fp: E.tensor_scalar(
                out=cols_sb[:, fp, 16:17], in0=cols_sb[:, fp, 2:3],
                scalar1=cols_sb[:, fp, 4:5], scalar2=-1.0, op0=OP.mult, op1=OP.mult))
            vop(lambda E, fp=fp: E.tensor_scalar(
                out=cols_sb[:, fp, 6:7], in0=cols_sb[:, fp, 0:1],
                scalar1=cols_sb[:, fp, 2:3], scalar2=None, op0=OP.subtract))
            vop(lambda E, fp=fp: E.tensor_scalar(
                out=cols_sb[:, fp, 7:8], in0=cols_sb[:, fp, 2:3],
                scalar1=cols_sb[:, fp, 1:2], scalar2=None, op0=OP.subtract))
            vdrain()
            vop(lambda E, fp=fp: E.tensor_tensor(
                out=cols_sb[:, fp, 9:10], in0=cols_sb[:, fp, 6:7],
                in1=cols_sb[:, fp, 7:8], op=OP.max))
            vdrain()
            vop(lambda E, fp=fp: E.tensor_scalar(
                out=cols_sb[:, fp, 8:9], in0=cols_sb[:, fp, 9:10],
                scalar1=cols_sb[:, fp, 4:5], scalar2=1e-12, op0=OP.mult, op1=OP.max))
            vdrain()
            vop(lambda E, fp=fp: E.tensor_scalar(
                out=cols_sb[:, fp, 8:9], in0=cols_sb[:, fp, 8:9],
                scalar1=1.0 / 127.0, scalar2=None, op0=OP.mult))
            vdrain()
            vop(lambda E, fp=fp: E.reciprocal(out=cols_sb[:, fp, 10:11], in_=cols_sb[:, fp, 8:9]))
            vdrain()
            v_rr = cnt["v"]
            wait("scalar", "v", v_rr)
            run("scalar", "s", lambda E, fp=fp, z1=z1: E.activation(
                out=z1, in_=z1, func=AF.Identity,
                bias=cols_sb[:, fp, 16:17], scale=cols_sb[:, fp, 4:5]))
            run("scalar", "s", lambda E, fp=fp, z1=z1, ql=ql: E.activation(
                out=ql, in_=z1, func=AF.Identity,
                bias=rnd_sb[:, :], scale=cols_sb[:, fp, 10:11]))
            run("scalar", "s", lambda E, ql=ql: E.activation(
                out=ql, in_=ql, func=AF.Identity,
                bias=nrnd_sb[:, :], scale=1.0))
            s_ql = cnt["s"]
            s_ql_l[0] = s_ql

        def stage2b(tc):
            fp = tc % 2
            z1 = ab_sb[:, fp, 0:1024]
            z2 = ab_sb[:, fp, 1024:2048]
            ql = h_sb[:, fp, 0:1024]
            wait("tensor", "s", s_ql_l[0])
            wait("tensor", "v", v_qlT_l[0])
            for g in range(8):
                run("tensor", "t", lambda E, g=g, ql=ql: E.transpose(
                    out=ps_bm[:, g * 128:(g + 1) * 128],
                    in_=ql[:, g * 128:(g + 1) * 128], identity=ident_sb[:, :]))
            t_trAll = cnt["t"]
            wait("vector", "t", t_trAll)
            run("vector", "v", lambda E: E.tensor_copy(
                out=qlT_sb[:, 0:8, :], in_=ps_bm[:, :]))
            vdrain()
            v_qlT = cnt["v"]
            v_qlT_l[0] = v_qlT
            wait("tensor", "v", v_qlT)
            for jb in range(2):
                for c in range(8):
                    run("tensor", "t", lambda E, c=c, jb=jb: E.matmul(
                        out=ps_cb[:, jb * 512:(jb + 1) * 512], lhsT=qlT_sb[:, c, :],
                        rhs=qrhs_sb[:, c, jb * 512:(jb + 1) * 512],
                        start=(c == 0), stop=(c == 7)))
                t_ffn = cnt["t"]
                wait("vector", "t", t_ffn)
                run("vector", "v", lambda E, jb=jb, fp=fp: E.scalar_tensor_tensor(
                    out=ffn_sb[:, jb * 512:(jb + 1) * 512], in0=ps_cb[:, jb * 512:(jb + 1) * 512],
                    scalar=cols_sb[:, fp, 8:9], in1=rs_bc[:, jb * 512:(jb + 1) * 512],
                    op0=OP.mult, op1=OP.mult))
                run("vector", "v", lambda E, jb=jb, z1=z1, z2=z2: E.tensor_tensor(
                    out=z2[:, jb * 512:(jb + 1) * 512], in0=z1[:, jb * 512:(jb + 1) * 512],
                    in1=ffn_sb[:, jb * 512:(jb + 1) * 512], op=OP.add))
                vop(lambda E, jb=jb, fp=fp, z2=z2: E.bn_stats(
                    out=bnst_sb[:, fp, jb * 6:(jb + 1) * 6],
                    in_=z2[:, jb * 512:(jb + 1) * 512]))
            vdrain()
            vop(lambda E, fp=fp: E.bn_aggr(out=cols_sb[:, fp, 12:14], in_=bnst_sb[:, fp, :]))
            vdrain()
            v_var2 = cnt["v"]
            wait("scalar", "v", v_var2)
            run("scalar", "s", lambda E, fp=fp: E.activation(
                out=cols_sb[:, fp, 15:16], in_=cols_sb[:, fp, 13:14], func=AF.Sqrt,
                bias=eps_sb[:, :], scale=1.0))
            s_sq2[fp] = cnt["s"]

        def stage3(tc):
            fp = tc % 2
            z2 = ab_sb[:, fp, 1024:2048]
            wait("vector", "s", s_sq2[fp])
            vop(lambda E, fp=fp: E.reciprocal(out=cols_sb[:, fp, 14:15], in_=cols_sb[:, fp, 15:16]))
            vdrain()
            vop(lambda E, fp=fp: E.tensor_scalar(
                out=cols_sb[:, fp, 17:18], in0=cols_sb[:, fp, 12:13],
                scalar1=cols_sb[:, fp, 14:15], scalar2=-1.0, op0=OP.mult, op1=OP.mult))
            vdrain()
            v_nb2 = cnt["v"]
            wait("scalar", "v", v_nb2)
            wait("scalar", "dout", dout_after_tc[fp])
            run("scalar", "s", lambda E, fp=fp, z2=z2: E.activation(
                out=u_sb[:, fp, 0:D], in_=z2, func=AF.Identity,
                bias=cols_sb[:, fp, 17:18], scale=cols_sb[:, fp, 14:15]))
            s_out = cnt["s"]
            wait("gpsimd", "s", s_out)
            run("gpsimd", "dout", lambda E, tc=tc, fp=fp: E.dma_start(
                out=out[tc * 128:(tc + 1) * 128, :], in_=u_sb[:, fp, 0:D]), 16)
            dout_after_tc[fp] = cnt["dout"]

        ffn_loads(0)
        ffn_loads(1)
        stage1(0)
        for tc in range(16):
            if tc + 2 < 16:
                ffn_loads(tc + 2)
            stage2a(tc)
            if tc + 1 < 16:
                stage1(tc + 1)
            if tc >= 1:
                stage3(tc - 1)
            stage2b(tc)
        stage3(15)
        wait("gpsimd", "dout", cnt["dout"])

        @blk_.sync
        def _(E):
            for f in ops["sync"]:
                f(E, SEMH)

        @blk_.tensor
        def _(E):
            for f in ops["tensor"]:
                f(E, SEMH)

        @blk_.scalar
        def _(E):
            for f in ops["scalar"]:
                f(E, SEMH)

        @blk_.vector
        def _(E):
            for f in ops["vector"]:
                f(E, SEMH)

        @blk_.gpsimd
        def _(E):
            for f in ops["gpsimd"]:
                f(E, SEMH)

    return nc


def kernel(L0_attn_x, L0_attn_W_xproj, L0_attn_W_dt, L0_attn_b_dt, L0_attn_A_log,
           L0_ffn_rhs):
    x = np.asarray(L0_attn_x, dtype=np.float32).reshape(T, D)
    wxp0 = np.asarray(L0_attn_W_xproj, dtype=np.float32)
    wxp = np.ascontiguousarray(
        np.concatenate([wxp0[:, 32:96], wxp0[:, 0:16], wxp0[:, 16:32]], axis=1))
    wdt = np.asarray(L0_attn_W_dt, dtype=np.float32)
    bdt = np.asarray(L0_attn_b_dt, dtype=np.float32)
    alog = np.asarray(L0_attn_A_log, dtype=np.float32)
    rhs = np.ascontiguousarray(np.asarray(L0_ffn_rhs, dtype=np.float32))
    rhsT = np.ascontiguousarray(rhs.T)
    xtf = np.ascontiguousarray(x.T)
    ident = np.eye(128, dtype=np.float32)
    bcsel = np.zeros((32, 32, 128), dtype=np.float32)
    bcsel[np.arange(32), np.arange(32), :] = 1.0

    nc = build_program()
    in_maps = []
    for k in range(NCORE):
        dsl = slice(k * 128, (k + 1) * 128)
        in_maps.append({
            "xtf": xtf,
            "xtd": np.ascontiguousarray(xtf[dsl, :]),
            "xown": np.ascontiguousarray(x[k * OWN:(k + 1) * OWN, :]),
            "wxp": wxp,
            "wdtd": np.ascontiguousarray(wdt[:, dsl]),
            "bdtd": np.ascontiguousarray(bdt[dsl].reshape(128, 1)),
            "alogd": np.ascontiguousarray(alog.T[dsl, :]),
            "rhs": rhs, "rhsT": rhsT, "ident": ident, "bcsel": bcsel,
        })
    res = run_bass_kernel_spmd(nc, in_maps, core_ids=list(range(NCORE)))
    shards = [res.results[c]["out"] for c in range(NCORE)]
    return np.concatenate(shards, axis=0).reshape(B, S, D).astype(np.float32)
